# version 04
# baseline (speedup 1.0000x reference)
import numpy as np
import ml_dtypes
from contextlib import ExitStack

import concourse.bass as bass
import concourse.mybir as mybir
from concourse.bass_utils import run_bass_kernel_spmd

F32 = mybir.dt.float32
BF16 = mybir.dt.bfloat16
AF = mybir.ActivationFunctionType
ALU = mybir.AluOpType
AX = mybir.AxisListType.X

S, D = 2048, 1024
NH, KVH, HD = 16, 4, 64
E, FF = 8, 4096
NTT, NTB, NKC = 16, 4, 8
EPS = 1e-5


def _host_tables():
    half = HD // 2
    try:
        import jax
        cpu = jax.devices("cpu")[0]
        import jax.numpy as jnp
        with jax.default_device(cpu):
            inv_freq = 1.0 / 10000.0 ** (jnp.arange(half, dtype=jnp.float32) / half)
            t = jnp.arange(S, dtype=jnp.float32)
            freqs = jnp.einsum("i,j->ij", t, inv_freq)
            emb = jnp.concatenate((freqs, freqs), axis=-1)
            cos = np.asarray(jnp.cos(emb), dtype=np.float32)
            sin = np.asarray(jnp.sin(emb), dtype=np.float32)
    except Exception:
        inv_freq = (1.0 / (np.float32(10000.0) **
                           (np.arange(half, dtype=np.float32) / np.float32(half)))).astype(np.float32)
        t = np.arange(S, dtype=np.float32)
        freqs = (t[:, None] * inv_freq[None, :]).astype(np.float32)
        emb = np.concatenate((freqs, freqs), axis=-1)
        cos = np.cos(emb).astype(np.float32)
        sin = np.sin(emb).astype(np.float32)
    cos_fm = cos.T.copy()
    sin_fm = sin.T.copy()
    ssin = sin_fm.copy()
    ssin[:32] = -sin_fm[:32]
    return (np.ascontiguousarray(np.concatenate([cos_fm, cos_fm], axis=0)),
            np.ascontiguousarray(np.concatenate([ssin, ssin], axis=0)))


def _host_masks():
    p = np.arange(128)[:, None]
    j = np.arange(512)[None, :]
    ms = [(128 * r + p <= j).astype(np.float32) for r in range(4)]
    return np.ascontiguousarray(np.concatenate(ms, axis=1).astype(ml_dtypes.bfloat16))


class St:
    def __init__(self):
        self.c = {}
        self.sems = {}
        self.bankg = {}

    def inc(self, instr, key, n=1):
        instr.then_inc(self.sems[key], n)
        self.c[key] = self.c.get(key, 0) + n
        return self.c[key]

    def cnt(self, key):
        return self.c.get(key, 0)

    def wait(self, eng, key, val):
        if val and val > 0:
            eng.wait_ge(self.sems[key], val)

    def bwait(self, eng, b):
        g = self.bankg.get(b)
        if g is not None:
            self.wait(eng, g[0], g[1])

    def bset(self, b, key, val):
        self.bankg[b] = (key, val)


def build_program(upto="full", dump=None):
    nc = bass.Bass("TRN2", target_bir_lowering=False)
    ap_x = nc.dram_tensor("x", [S, D], F32, kind="ExternalInput").ap()
    ap_wq = nc.dram_tensor("Wq", [D, D], F32, kind="ExternalInput").ap()
    ap_wk = nc.dram_tensor("Wk", [D, 256], F32, kind="ExternalInput").ap()
    ap_wv = nc.dram_tensor("Wv", [D, 256], F32, kind="ExternalInput").ap()
    ap_wo = nc.dram_tensor("Wo", [D, D], F32, kind="ExternalInput").ap()
    ap_gwT = nc.dram_tensor("gwT", [D, E], F32, kind="ExternalInput").ap()
    ap_w1 = nc.dram_tensor("W1", [E + 1, D, FF], F32, kind="ExternalInput").ap()
    ap_w2 = nc.dram_tensor("W2", [E + 1, FF, D], F32, kind="ExternalInput").ap()
    ap_b1 = nc.dram_tensor("b1c", [128, (E + 1) * 32], F32, kind="ExternalInput").ap()
    ap_b2aug = nc.dram_tensor("b2aug", [E + 1, D], F32, kind="ExternalInput").ap()
    ap_cos = nc.dram_tensor("costab", [128, S], F32, kind="ExternalInput").ap()
    ap_ssin = nc.dram_tensor("ssintab", [128, S], F32, kind="ExternalInput").ap()
    ap_masks = nc.dram_tensor("masks", [128, 2048], BF16, kind="ExternalInput").ap()
    ap_eye = nc.dram_tensor("eye", [128, 128], F32, kind="ExternalInput").ap()
    ap_consts = nc.dram_tensor("consts", [128, 8], F32, kind="ExternalInput").ap()
    ap_out = nc.dram_tensor("out", [S, D], F32, kind="ExternalOutput").ap()
    ap_x1s = nc.dram_tensor("x1scr", [S, D], F32, kind="Internal").ap()
    ap_facc = nc.dram_tensor("ffnacc", [S, D], F32, kind="Internal").ap()
    ap_dbg = None
    if dump is not None:
        ap_dbg = nc.dram_tensor("dbg", list(dump[1]), dump[2], kind="ExternalOutput").ap()

    st = St()
    V = st.c

    with ExitStack() as stack:
        en = stack.enter_context
        sb = lambda n, sh, dt=F32: en(nc.sbuf_tensor(n, sh, dt))

        consts = sb("c_consts", [128, 8])
        eye = sb("c_eye", [128, 128])
        b1_sb = sb("c_b1", [128, (E + 1) * 32])
        b2aug_sb = sb("c_b2aug", [E + 1, D])
        gwT_sb = sb("c_gwT", [128, NKC * E])
        ones_sb = sb("c_ones", [1, 64])
        xfm = sb("p_xfm", [128, NKC * S], BF16)
        comb_sb = sb("p_comb", [128, NTT * (E + 1)])
        cT_sb = sb("p_cT", [E + 1, S])
        class _LnScr:
            def __init__(self, i):
                self.st6 = sb(f"ln{i}_st6", [128, 12])
                self.st2a = sb(f"ln{i}_st2a", [128, 2])
                self.st2b = sb(f"ln{i}_st2b", [128, 2])
                self.tmp2 = sb(f"ln{i}_tmp2", [128, 2])
                self.d1 = sb(f"ln{i}_d1", [128, 1])
                self.var1 = sb(f"ln{i}_var1", [128, 1])
                self.sstd = sb(f"ln{i}_sstd", [128, 1])
                self.sr = sb(f"ln{i}_sr", [128, 1])
        lnS = [_LnScr(0), _LnScr(1)]
        xring = [sb(f"r_x{i}", [128, D]) for i in range(2)]
        lring = [sb(f"r_ln{i}", [128, D]) for i in range(2)]

        s_bc = ExitStack()
        bcb = lambda n, sh, dt=F32: s_bc.enter_context(nc.sbuf_tensor(n, sh, dt))
        wo_sb = bcb("a_wo", [128, NKC * D], BF16)
        q_sb = bcb("a_q", [128, 8 * S], BF16)
        k2_sb = bcb("a_k2", [128, KVH * S], BF16)
        v_sb = bcb("a_v", [128, NTT * KVH * 65], BF16)

        s_b = ExitStack()
        bb = lambda n, sh, dt=F32: s_b.enter_context(nc.sbuf_tensor(n, sh, dt))
        costab = bb("a_cos", [128, S])
        ssintab = bb("a_ssin", [128, S])
        wq_sb = bb("a_wq", [128, NKC * D], BF16)
        wk_sb = bb("a_wk", [128, NKC * 256], BF16)
        wv_sb = bb("a_wv", [128, NKC * 256], BF16)
        qraw = [bb(f"r_qraw{i}", [128, S]) for i in range(2)]
        qswp = [bb(f"r_qswp{i}", [128, S]) for i in range(2)]

        psum = en(nc.psum_tensor("psum", [128, 4096], F32))
        BANK = lambda b, w=512: psum[:, b * 512: b * 512 + w]

        for name in ["ld0", "ld1", "ld2", "ld3", "st_o", "ws0", "ws1", "cst",
                     "swp0", "swp1", "sp0", "sp1", "sp2", "fa0", "fa1", "fa2", "fa3",
                     "lf0", "lf1", "lx0", "lx1", "so0", "so1", "cv", "cs", "ct", "cg"]:
            st.sems[name] = en(nc.semaphore(name))

        nc.const_aps.register(F32, EPS, consts[:, 0:1])
        nc.const_aps.register(F32, 0.125, consts[:, 1:2])
        nc.const_aps.register(F32, -10.0, consts[:, 2:3])
        nc.const_aps.register(F32, 1.0, consts[:, 3:4])
        nc.const_aps.register(F32, 0.0, consts[:, 4:5])
        nc.const_aps.register(F32, -0.5, consts[:, 5:6])

        block = en(nc.Block())

        @block.scalar
        def _(scalar):
            st.inc(scalar.dma_start(out=consts[:], in_=ap_consts[:, :]), "cst", 16)
            st.inc(scalar.dma_start(out=eye[:], in_=ap_eye[:, :]), "cst", 16)
            st.inc(scalar.dma_start(out=costab[:], in_=ap_cos[:, :]), "cst", 16)
            st.inc(scalar.dma_start(out=ssintab[:], in_=ap_ssin[:, :]), "cst", 16)
            st.inc(scalar.dma_start(out=b1_sb[:], in_=ap_b1[:, :]), "cst", 16)
            st.inc(scalar.dma_start(out=b2aug_sb[:], in_=ap_b2aug[:, :]), "cst", 16)
            st.inc(scalar.dma_start(out=gwT_sb[:].rearrange("p (kc n) -> p kc n", n=E),
                                    in_=ap_gwT.rearrange("(kc p) n -> p kc n", p=128)),
                   "cst", 16)
        V["CST_ALL"] = st.cnt("cst")

        @block.gpsimd
        def _(gpsimd):
            gpsimd.memset(ones_sb[0:1, :], 1.0)
            gpsimd.memset(
                comb_sb[:].rearrange("p (tt e) -> p tt e", e=E + 1)[:, :, E:E + 1], 1.0)
            st.inc(gpsimd.memset(
                v_sb[:].rearrange("p (tt k d) -> p tt k d", k=KVH, d=65)[:, :, :, 64:65],
                1.0), "cg", 1)
            st.inc(gpsimd.dma_start(
                out=wq_sb[:].rearrange("p (kc n) -> p kc n", n=D),
                in_=ap_wq.rearrange("(kc p) n -> p kc n", p=128)), "ws0", 16)
            st.inc(gpsimd.dma_start(
                out=wk_sb[:].rearrange("p (kc n) -> p kc n", n=256),
                in_=ap_wk.rearrange("(kc p) n -> p kc n", p=128)), "ws0", 16)
            st.inc(gpsimd.dma_start(
                out=wv_sb[:].rearrange("p (kc n) -> p kc n", n=256),
                in_=ap_wv.rearrange("(kc p) n -> p kc n", p=128)), "ws0", 16)
            st.inc(gpsimd.dma_start(
                out=wo_sb[:].rearrange("p (kc n) -> p kc n", n=D),
                in_=ap_wo.rearrange("(kc p) n -> p kc n", p=128)), "ws0", 16)
        V["CG_INIT"] = st.cnt("cg")
        V["WS_QKVO"] = st.cnt("ws0")

        @block.scalar
        def _(scalar):
            st.wait(scalar, "cst", V["CST_ALL"])

        ln_guard = {}

        def ln_stats(p_, tile_ap, tag, pre=None):
            L = lnS[p_]

            @block.vector
            def _(vector):
                g = ln_guard.get(p_)
                if g is not None:
                    st.wait(vector, "cs", g)
                if pre is not None:
                    pre(vector)
                vector.drain()
                vector.bn_stats(L.st6[:, 0:6], tile_ap[:, 0:512])
                vector.bn_stats(L.st6[:, 6:12], tile_ap[:, 512:1024])
                vector.drain()
                vector.bn_aggr(L.st2a[:, :], L.st6[:, 0:6])
                vector.bn_aggr(L.st2b[:, :], L.st6[:, 6:12])
                vector.drain()
                vector.tensor_tensor(L.tmp2[:, :], L.st2a[:, :], L.st2b[:, :], ALU.add)
                vector.tensor_tensor(L.d1[:, :], L.st2a[:, 0:1], L.st2b[:, 0:1],
                                     ALU.subtract)
                vector.drain()
                vector.tensor_scalar(out=L.tmp2[:, :], in0=L.tmp2[:, :], scalar1=0.5,
                                     scalar2=None, op0=ALU.mult)
                vector.tensor_tensor(L.d1[:, :], L.d1[:, :], L.d1[:, :], ALU.mult)
                vector.drain()
                st.inc(vector.scalar_tensor_tensor(
                    out=L.var1[:, :], in0=L.d1[:, :], scalar=0.25,
                    in1=L.tmp2[:, 1:2], op0=ALU.mult, op1=ALU.add), "cv")
            V[f"{tag}_var"] = st.cnt("cv")

        def ln_rstd(p_, tag):
            L = lnS[p_]

            @block.scalar
            def _(scalar):
                st.wait(scalar, "cv", V[f"{tag}_var"])
                scalar.activation(out=L.sstd[:, :], in_=L.var1[:, :],
                                  func=AF.Ln, bias=EPS, scale=1.0)
                scalar.drain()
                st.inc(scalar.activation(out=L.sr[:, :], in_=L.sstd[:, :],
                                         func=AF.Exp, bias=0.0, scale=-0.5), "cs")
            V[f"{tag}_rstd"] = st.cnt("cs")
            ln_guard[p_] = st.cnt("cs")

        def ln_norm(p_, tile_ap, out_ap, tag, pre=None):
            L = lnS[p_]

            @block.vector
            def _(vector):
                st.wait(vector, "cs", V[f"{tag}_rstd"])
                if pre is not None:
                    pre(vector)
                st.inc(vector.tensor_scalar(
                    out=out_ap, in0=tile_ap, scalar1=L.tmp2[:, 0:1],
                    scalar2=L.sr[:, :], op0=ALU.subtract, op1=ALU.mult), "cv")
            V[f"{tag}_ln"] = st.cnt("cv")

        def a_tail(u):
            def pre(vector, u=u):
                if u >= 2:
                    st.wait(vector, "ct", V[f"A_tp_{u - 2}"])
            ln_norm(u % 2, xring[u % 2][:, :], lring[u % 2][:, :], f"A{u}", pre)
            for kc in range(NKC):
                @block.tensor
                def _(tensor, u=u, kc=kc):
                    if kc == 0:
                        st.wait(tensor, "cv", V[f"A{u}_ln"])
                    b = kc % 4
                    st.bwait(tensor, b)
                    st.inc(tensor.transpose(BANK(b, 128),
                                            lring[u % 2][:, kc * 128:(kc + 1) * 128],
                                            eye[:, :]), "ct")
                    V[f"A_tp_{u}_{kc}"] = st.cnt("ct")

                @block.vector
                def _(vector, u=u, kc=kc):
                    st.wait(vector, "ct", V[f"A_tp_{u}_{kc}"])
                    cp = vector.tensor_copy(
                        xfm[:, kc * S + u * 128: kc * S + (u + 1) * 128],
                        BANK(kc % 4, 128))
                    st.inc(cp, "cv")
                    st.bset(kc % 4, "cv", st.cnt("cv"))
            V[f"A_tp_{u}"] = st.cnt("ct")

        for tt in range(NTT + 1):
            if tt < NTT:
                sem = f"ld{tt % 2}"

                @block.sync
                def _(sync, tt=tt, sem=sem):
                    if tt >= 2:
                        st.wait(sync, "cv", V[f"A{tt - 2}_ln"])
                    st.inc(sync.dma_start(out=xring[tt % 2][:],
                                          in_=ap_x[tt * 128:(tt + 1) * 128, :]),
                           sem, 16)
                V[f"A_ld_{tt}"] = st.cnt(sem)

                def pre(vector, tt=tt, sem=sem):
                    st.wait(vector, sem, V[f"A_ld_{tt}"])
                ln_stats(tt % 2, xring[tt % 2][:, :], f"A{tt}", pre)
                ln_rstd(tt % 2, f"A{tt}")
            if tt >= 1:
                a_tail(tt - 1)
        V["A_DONE"] = st.cnt("cv")

        if upto == "A":
            _finish(nc, block, st, ap_dbg, dump, locals(), ap_out, xring)
            s_b.close(); s_bc.close()
            return nc

        def emit_rope(c):
            is_k = c >= 8
            sl = c % 2

            @block.vector
            def _(vector, c=c, is_k=is_k, sl=sl):
                st.wait(vector, f"swp{c % 2}", V[f"B_sw_{c}"])
                if c == 0:
                    st.wait(vector, "cst", V["CST_ALL"])
                vector.drain()
                vector.tensor_tensor(qraw[sl][:, :], qraw[sl][:, :], costab[:, :],
                                     ALU.mult)
                vector.tensor_tensor(qswp[sl][:, :], qswp[sl][:, :], ssintab[:, :],
                                     ALU.mult)
                vector.drain()
                if not is_k:
                    st.inc(vector.tensor_tensor(q_sb[:, c * S:(c + 1) * S],
                                                qraw[sl][:, :], qswp[sl][:, :],
                                                ALU.add), "cv")
                else:
                    for half, kvh in ((0, 2 * (c - 8)), (1, 2 * (c - 8) + 1)):
                        lo = half * 64
                        vector.tensor_tensor(k2_sb[0:64, kvh * S:(kvh + 1) * S],
                                             qraw[sl][lo:lo + 64, :],
                                             qswp[sl][lo:lo + 64, :], ALU.add)
                        cp = vector.tensor_tensor(k2_sb[64:128, kvh * S:(kvh + 1) * S],
                                                  qraw[sl][lo:lo + 64, :],
                                                  qswp[sl][lo:lo + 64, :], ALU.add)
                    st.inc(cp, "cv")
            V[f"B_rope_{c}"] = st.cnt("cv")

        for c in range(10):
            is_k = c >= 8
            sl = c % 2
            @block.tensor
            def _(tensor, c=c, is_k=is_k):
                if c == 0:
                    st.wait(tensor, "ws0", V["WS_QKVO"])
                    st.wait(tensor, "cv", V["A_DONE"])
                for tb in range(NTB):
                    b = 4 + tb
                    st.bwait(tensor, b)
                    for kc in range(NKC):
                        if is_k:
                            lhs = wk_sb[:, kc * 256 + (c - 8) * 128: kc * 256 + (c - 7) * 128]
                        else:
                            lhs = wq_sb[:, kc * D + c * 128: kc * D + (c + 1) * 128]
                        mm = tensor.matmul(BANK(b),
                                           lhs,
                                           xfm[:, kc * S + tb * 512: kc * S + (tb + 1) * 512],
                                           start=(kc == 0), stop=(kc == NKC - 1))
                    st.inc(mm, "ct")
                    V[f"B_mm_{c}_{tb}"] = st.cnt("ct")

            @block.vector
            def _(vector, c=c, sl=sl):
                vector.drain()
                for tb in range(NTB):
                    st.wait(vector, "ct", V[f"B_mm_{c}_{tb}"])
                    cp = vector.tensor_copy(qraw[sl][:, tb * 512:(tb + 1) * 512],
                                            BANK(4 + tb))
                    st.inc(cp, "cv")
                    st.bset(4 + tb, "cv", st.cnt("cv"))
            V[f"B_ev_{c}"] = st.cnt("cv")

            @block.scalar
            def _(scalar, c=c, sl=sl):
                st.wait(scalar, "cv", V[f"B_ev_{c}"])
                sw = st.sems[f"swp{c % 2}"]
                scalar.dma_start(out=qswp[sl][0:32], in_=qraw[sl][32:64]).then_inc(sw, 16)
                scalar.dma_start(out=qswp[sl][32:64], in_=qraw[sl][0:32]).then_inc(sw, 16)
                scalar.dma_start(out=qswp[sl][64:96], in_=qraw[sl][96:128]).then_inc(sw, 16)
                scalar.dma_start(out=qswp[sl][96:128], in_=qraw[sl][64:96]).then_inc(sw, 16)
            st.c[f"swp{c % 2}"] = st.cnt(f"swp{c % 2}") + 64
            V[f"B_sw_{c}"] = st.cnt(f"swp{c % 2}")

            if c >= 1:
                emit_rope(c - 1)
        emit_rope(9)

        for tt in range(NTT):
            @block.tensor
            def _(tensor, tt=tt):
                b = 4 + tt % 4
                st.bwait(tensor, b)
                for kc in range(NKC):
                    mm = tensor.matmul(BANK(b, 256),
                                       xfm[:, kc * S + tt * 128: kc * S + (tt + 1) * 128],
                                       wv_sb[:, kc * 256:(kc + 1) * 256],
                                       start=(kc == 0), stop=(kc == NKC - 1))
                st.inc(mm, "ct")
                V[f"B_vmm_{tt}"] = st.cnt("ct")

            @block.vector
            def _(vector, tt=tt):
                st.wait(vector, "ct", V[f"B_vmm_{tt}"])
                if tt == 0:
                    st.wait(vector, "cg", V["CG_INIT"])
                for kvh in range(KVH):
                    cp = vector.tensor_copy(
                        v_sb[:, tt * KVH * 65 + kvh * 65: tt * KVH * 65 + kvh * 65 + 64],
                        BANK(4 + tt % 4, 256)[:, kvh * 64:(kvh + 1) * 64])
                st.inc(cp, "cv")
                st.bset(4 + tt % 4, "cv", st.cnt("cv"))
        V["B_DONE"] = st.cnt("cv")

        if upto == "B":
            _finish(nc, block, st, ap_dbg, dump, locals(), ap_out, xring)
            s_b.close(); s_bc.close()
            return nc

        s_b.close()
        s_cd = ExitStack()
        yall = s_cd.enter_context(nc.sbuf_tensor("a_yall", [128, 8 * S], BF16))
        s_c2 = ExitStack()
        c2b = lambda n, sh, dt=F32: s_c2.enter_context(nc.sbuf_tensor(n, sh, dt))
        masks_sb = c2b("a_masks", [128, 2048], BF16)
        attw = c2b("a_attw", [128, 16 * 1024], BF16)
        zstr = lring[0]
        ytmp_att = lring[1]

        @block.scalar
        def _(scalar):
            st.inc(scalar.dma_start(out=masks_sb[:], in_=ap_masks[:, :]), "cst", 16)
        V["CST2"] = st.cnt("cst")

        sc_i = 0
        pending_zb = []
        pair_guard = {}
        for hp in range(8):
            kvh = hp // 2
            for qb in range(NTB):
                n_kc = 4 * (qb + 1)
                kcs = list(range(4 * qb, 4 * qb + 4)) + list(range(0, 4 * qb))
                ct0 = st.cnt("ct")
                cs0 = st.cnt("cs")
                st.c["ct"] = ct0 + n_kc
                st.c["cs"] = cs0 + n_kc
                for i, kc in enumerate(kcs):
                    V[f"C_sc_{hp}_{qb}_{kc}"] = ct0 + i + 1
                    V[f"C_exp_{hp}_{qb}_{kc}"] = cs0 + i + 1

                @block.tensor
                def _(tensor, hp=hp, qb=qb, kcs=kcs, kvh=kvh, ct0=ct0, cs0=cs0):
                    if hp == 0 and qb == 0:
                        st.wait(tensor, "cv", V["B_DONE"])
                        st.wait(tensor, "cst", V["CST_ALL"])
                    for i, kc in enumerate(kcs):
                        pair = (sc_i + i) % 2
                        b0 = 0 if pair == 0 else 2
                        if i >= 2:
                            tensor.wait_ge(st.sems["cs"], cs0 + i - 1)
                        elif pair_guard.get(pair) is not None:
                            tensor.wait_ge(st.sems["cs"], pair_guard[pair])
                        tensor.matmul(BANK(b0),
                                      k2_sb[0:64, kvh * S + kc * 128: kvh * S + (kc + 1) * 128],
                                      q_sb[0:64, hp * S + qb * 512: hp * S + (qb + 1) * 512],
                                      start=True, stop=True, tile_position=(0, 0))
                        tensor.matmul(
                            BANK(b0 + 1),
                            k2_sb[64:128, kvh * S + kc * 128: kvh * S + (kc + 1) * 128],
                            q_sb[64:128, hp * S + qb * 512: hp * S + (qb + 1) * 512],
                            start=True, stop=True,
                            tile_position=(64, 0)).then_inc(st.sems["ct"], 1)

                @block.scalar
                def _(scalar, hp=hp, qb=qb, kcs=kcs, ct0=ct0):
                    pv = V.get("C_prev_attv")
                    if pv is not None:
                        st.wait(scalar, "ct", pv)
                    for i, kc in enumerate(kcs):
                        pair = (sc_i + i) % 2
                        b0 = 0 if pair == 0 else 2
                        scalar.wait_ge(st.sems["ct"], ct0 + i + 1)
                        scalar.activation(
                            out=attw[:, kc * 1024:(kc + 1) * 1024],
                            in_=psum[:, b0 * 512: b0 * 512 + 1024],
                            func=AF.Exp, bias=-10.0,
                            scale=0.125).then_inc(st.sems["cs"], 1)

                pair_guard[(sc_i + n_kc - 2) % 2] = cs0 + n_kc - 1
                pair_guard[(sc_i + n_kc - 1) % 2] = cs0 + n_kc
                sc_i += n_kc

                @block.vector
                def _(vector, hp=hp, qb=qb):
                    st.wait(vector, "cst", V["CST2"])
                    st.wait(vector, "cs", V[f"C_exp_{hp}_{qb}_{4 * qb + 3}"])
                    for r in range(4):
                        kc = 4 * qb + r
                        for hh in range(2):
                            mk = vector.tensor_tensor(
                                attw[:, kc * 1024 + hh * 512: kc * 1024 + hh * 512 + 512],
                                attw[:, kc * 1024 + hh * 512: kc * 1024 + hh * 512 + 512],
                                masks_sb[:, r * 512:(r + 1) * 512], ALU.mult)
                    st.inc(mk, "cv")
                V[f"C_mask_{hp}_{qb}"] = st.cnt("cv")

                last_kc = kcs[-1]
                if pending_zb:
                    fn = pending_zb.pop(0)
                    fn()
                for hh in range(2):
                    yb = 4 + hh
                    @block.tensor
                    def _(tensor, hp=hp, qb=qb, hh=hh, n_kc=n_kc, yb=yb,
                          last_kc=last_kc, kvh=kvh):
                        st.wait(tensor, "cv", V[f"C_mask_{hp}_{qb}"])
                        st.wait(tensor, "cs", V[f"C_exp_{hp}_{qb}_{last_kc}"])
                        st.bwait(tensor, yb)
                        for i_kc in range(n_kc):
                            mm = tensor.matmul(
                                BANK(yb)[0:65, :],
                                v_sb[:, i_kc * KVH * 65 + kvh * 65: i_kc * KVH * 65 + kvh * 65 + 65],
                                attw[:, i_kc * 1024 + hh * 512: i_kc * 1024 + hh * 512 + 512],
                                start=(i_kc == 0), stop=(i_kc == n_kc - 1))
                        st.inc(mm, "ct")
                    V[f"C_y_{hp}_{qb}_{hh}"] = st.cnt("ct")

                    @block.vector
                    def _(vector, hp=hp, qb=qb, hh=hh, yb=yb):
                        st.wait(vector, "ct", V[f"C_y_{hp}_{qb}_{hh}"])
                        g = st.bankg.get(100 + hh)
                        if g is not None:
                            st.wait(vector, g[0], g[1])
                        vector.drain()
                        vector.tensor_copy(zstr[0:1, hh * 512:(hh + 1) * 512],
                                           BANK(yb)[64:65, :])
                        vector.drain()
                        vector.reciprocal(zstr[0:1, hh * 512:(hh + 1) * 512],
                                          zstr[0:1, hh * 512:(hh + 1) * 512])
                        cp = vector.tensor_copy(ytmp_att[0:64, hh * 512:(hh + 1) * 512],
                                                BANK(yb)[0:64, :])
                        st.inc(cp, "cv")
                        st.bset(yb, "cv", st.cnt("cv"))
                    V[f"C_z_{hp}_{qb}_{hh}"] = st.cnt("cv")

                def make_zb(hp=hp, qb=qb):
                    def emit():
                        for hh in range(2):
                            @block.tensor
                            def _(tensor, hh=hh):
                                st.wait(tensor, "cv", V[f"C_z_{hp}_{qb}_{hh}"])
                                st.bwait(tensor, 6)
                                st.inc(tensor.matmul(
                                    BANK(6)[0:64, :], ones_sb[0:1, 0:64],
                                    zstr[0:1, hh * 512:(hh + 1) * 512],
                                    start=True, stop=True), "ct")
                            V[f"C_zb_{hp}_{qb}_{hh}"] = st.cnt("ct")
                            st.bset(100 + hh, "ct", st.cnt("ct"))

                            @block.vector
                            def _(vector, hh=hh):
                                st.wait(vector, "ct", V[f"C_zb_{hp}_{qb}_{hh}"])
                                st.inc(vector.tensor_tensor(
                                    yall[hh * 64: hh * 64 + 64,
                                         hp * S + qb * 512: hp * S + (qb + 1) * 512],
                                    ytmp_att[0:64, hh * 512:(hh + 1) * 512],
                                    BANK(6)[0:64, :], ALU.mult), "cv")
                                st.bset(6, "cv", st.cnt("cv"))
                    return emit
                pending_zb.append(make_zb())
                V["C_prev_attv"] = st.cnt("ct")
        while pending_zb:
            pending_zb.pop(0)()
        V["C_DONE"] = st.cnt("cv")

        if upto == "C":
            _finish(nc, block, st, ap_dbg, dump, locals(), ap_out, xring)
            s_c2.close(); s_cd.close(); s_bc.close()
            return nc

        s_c2.close()
        s_d = ExitStack()
        db_ = lambda n, sh, dt=F32: s_d.enter_context(nc.sbuf_tensor(n, sh, dt))
        x1t = [db_(f"d_x1_{i}", [128, D]) for i in range(3)]
        gring8 = db_("d_g8", [128, D])
        gev = db_("d_gev", [128, E])
        d_m1 = db_("d_m1", [128, 1])
        d_negm1 = db_("d_negm1", [128, 1])
        d_ismax = db_("d_ismax", [128, E])
        d_l2 = db_("d_l2", [128, E])
        d_m2 = db_("d_m2", [128, 1])
        d_expw = db_("d_expw", [128, E])
        d_maskw = db_("d_maskw", [128, E])
        d_wm = db_("d_wm", [128, E])
        d_zs = db_("d_zs", [128, 1])
        d_zr = db_("d_zr", [128, 1])
        ln3o = [lring[1], db_("d_ln3b", [128, D])]

        def d_head(tt):
            sem = f"ld{tt % 2}"

            @block.sync
            def _(sync, tt=tt, sem=sem):
                if tt == 0:
                    st.wait(sync, "cv", V["C_DONE"])
                if tt >= 2:
                    st.wait(sync, "cv", V[f"D_x1_{tt - 2}"])
                st.inc(sync.dma_start(out=xring[tt % 2][:],
                                      in_=ap_x[tt * 128:(tt + 1) * 128, :]), sem, 16)
            V[f"D_ld_{tt}"] = st.cnt(sem)

            ob = 0 if tt % 2 == 0 else 2
            @block.tensor
            def _(tensor, tt=tt, ob=ob):
                if tt == 0:
                    st.wait(tensor, "cv", V["C_DONE"])
                st.bwait(tensor, ob)
                st.bwait(tensor, ob + 1)
                for db in range(2):
                    for kc in range(NKC):
                        mm = tensor.matmul(
                            BANK(ob + db),
                            yall[:, kc * S + tt * 128: kc * S + (tt + 1) * 128],
                            wo_sb[:, kc * D + db * 512: kc * D + db * 512 + 512],
                            start=(kc == 0), stop=(kc == NKC - 1))
                st.inc(mm, "ct")
            V[f"D_omm_{tt}"] = st.cnt("ct")

            def preo(vector, tt=tt):
                st.wait(vector, "ct", V[f"D_omm_{tt}"])
            ln_stats(tt % 2, psum[:, ob * 512: ob * 512 + 1024], f"D2_{tt}", preo)
            ln_rstd(tt % 2, f"D2_{tt}")

        def d_tail(u):
            pu = u % 2
            ob = 0 if u % 2 == 0 else 2
            attn_ap = psum[:, ob * 512: ob * 512 + 1024]
            L = lnS[pu]

            @block.vector
            def _(vector, u=u, pu=pu, attn_ap=attn_ap, L=L):
                st.wait(vector, "cs", V[f"D2_{u}_rstd"])
                st.wait(vector, f"ld{u % 2}", V[f"D_ld_{u}"])
                if u >= 3:
                    st.wait(vector, f"sp{u % 3}", V[f"D_sp_{u - 3}"])
                vector.tensor_scalar(out=lring[0][:, :], in0=attn_ap,
                                     scalar1=L.tmp2[:, 0:1], scalar2=L.sr[:, :],
                                     op0=ALU.subtract, op1=ALU.mult)
                vector.drain()
                st.inc(vector.tensor_tensor(x1t[u % 3][:, :], lring[0][:, :],
                                            xring[u % 2][:, :], ALU.add), "cv")
            V[f"D_x1_{u}"] = st.cnt("cv")
            st.bset(ob, "cv", st.cnt("cv"))
            st.bset(ob + 1, "cv", st.cnt("cv"))

            @block.scalar
            def _(scalar, u=u):
                st.wait(scalar, "cv", V[f"D_x1_{u}"])
                st.inc(scalar.dma_start(out=ap_x1s[u * 128:(u + 1) * 128, :],
                                        in_=x1t[u % 3][:, :]), f"sp{u % 3}", 16)
            V[f"D_sp_{u}"] = st.cnt(f"sp{u % 3}")

            ln_stats(pu, x1t[u % 3][:, :], f"D3_{u}", None)
            ln_rstd(pu, f"D3_{u}")

            def pre3(vector, u=u):
                if u >= 2:
                    st.wait(vector, "ct", V[f"D_tpall_{u - 2}"])
            ln_norm(pu, x1t[u % 3][:, :], ln3o[pu][:, :], f"D3_{u}", pre3)

            for kc in range(NKC):
                @block.tensor
                def _(tensor, u=u, kc=kc, pu=pu):
                    if kc == 0:
                        st.wait(tensor, "cv", V[f"D3_{u}_ln"])
                    b = 4 + kc % 2
                    st.bwait(tensor, b)
                    st.inc(tensor.transpose(BANK(b, 128),
                                            ln3o[pu][:, kc * 128:(kc + 1) * 128],
                                            eye[:, :]), "ct")
                    V[f"D_tp_{u}_{kc}"] = st.cnt("ct")

                @block.vector
                def _(vector, u=u, kc=kc):
                    if kc == 0 and u >= 1:
                        st.wait(vector, "ct", V[f"D_gmm_{u - 1}"])
                    st.wait(vector, "ct", V[f"D_tp_{u}_{kc}"])
                    vector.tensor_copy(
                        xfm[:, kc * S + u * 128: kc * S + (u + 1) * 128],
                        BANK(4 + kc % 2, 128))
                    cp = vector.tensor_copy(gring8[:, kc * 128:(kc + 1) * 128],
                                            BANK(4 + kc % 2, 128))
                    st.inc(cp, "cv")
                    st.bset(4 + kc % 2, "cv", st.cnt("cv"))
                V[f"D_tpev_{u}_{kc}"] = st.cnt("cv")
            V[f"D_tpall_{u}"] = st.cnt("ct")
            V[f"D_tpev_{u}"] = st.cnt("cv")

            @block.tensor
            def _(tensor, u=u):
                st.wait(tensor, "cv", V[f"D_tpev_{u}"])
                st.bwait(tensor, 6)
                for kc in range(NKC):
                    mm = tensor.matmul(BANK(6, 8), gring8[:, kc * 128:(kc + 1) * 128],
                                       gwT_sb[:, kc * E:(kc + 1) * E],
                                       start=(kc == 0), stop=(kc == NKC - 1))
                st.inc(mm, "ct")
            V[f"D_gmm_{u}"] = st.cnt("ct")

            @block.vector
            def _(vector, u=u):
                st.wait(vector, "ct", V[f"D_gmm_{u}"])
                vector.drain()
                vector.tensor_copy(gev[:, :], BANK(6, 8))
                vector.drain()
                vector.reduce_max(d_m1[:, :], gev[:, :], axis=AX)
                vector.drain()
                vector.tensor_scalar(out=d_ismax[:, :], in0=gev[:, :],
                                     scalar1=d_m1[:, :], scalar2=None, op0=ALU.is_ge)
                vector.tensor_scalar(out=d_negm1[:, :], in0=d_m1[:, :],
                                     scalar1=-1.0, scalar2=None, op0=ALU.mult)
                vector.drain()
                vector.scalar_tensor_tensor(out=d_l2[:, :], in0=d_ismax[:, :],
                                            scalar=-1e30, in1=gev[:, :],
                                            op0=ALU.mult, op1=ALU.add)
                vector.drain()
                vector.reduce_max(d_m2[:, :], d_l2[:, :], axis=AX)
                vector.drain()
                st.inc(vector.tensor_scalar(out=d_maskw[:, :], in0=gev[:, :],
                                            scalar1=d_m2[:, :], scalar2=None,
                                            op0=ALU.is_ge), "cv")
                st.bset(6, "cv", st.cnt("cv"))
            V[f"D_cm1_{u}"] = st.cnt("cv")

            @block.scalar
            def _(scalar, u=u):
                st.wait(scalar, "cv", V[f"D_cm1_{u}"])
                st.inc(scalar.activation(out=d_expw[:, :], in_=gev[:, :], func=AF.Exp,
                                         bias=d_negm1[:, :], scale=1.0), "cs")
            V[f"D_cexp_{u}"] = st.cnt("cs")

            @block.vector
            def _(vector, u=u):
                st.wait(vector, "cs", V[f"D_cexp_{u}"])
                vector.tensor_tensor(d_wm[:, :], d_expw[:, :], d_maskw[:, :], ALU.mult)
                vector.drain()
                vector.reduce_sum(d_zs[:, :], d_wm[:, :], axis=AX)
                vector.drain()
                vector.reciprocal(d_zr[:, :], d_zs[:, :])
                vector.drain()
                st.inc(vector.tensor_scalar(
                    out=comb_sb[:, u * (E + 1): u * (E + 1) + E],
                    in0=d_wm[:, :], scalar1=d_zr[:, :], scalar2=None,
                    op0=ALU.mult), "cv")
            V[f"D_comb_{u}"] = st.cnt("cv")
            V[f"D_gev_{u}"] = st.cnt("cv")

        for tt in range(NTT + 1):
            if tt < NTT:
                d_head(tt)
            if tt >= 1:
                d_tail(tt - 1)

        for tt in range(NTT):
            @block.tensor
            def _(tensor, tt=tt):
                if tt == 0:
                    st.wait(tensor, "cv", V[f"D_comb_{NTT - 1}"])
                    st.wait(tensor, "cg", V["CG_INIT"])
                st.bwait(tensor, 7)
                st.inc(tensor.transpose(BANK(7, 128)[0:E + 1, :],
                                        comb_sb[:, tt * (E + 1):(tt + 1) * (E + 1)],
                                        eye[:, :]), "ct")
            V[f"D_cT_{tt}"] = st.cnt("ct")

            @block.vector
            def _(vector, tt=tt):
                st.wait(vector, "ct", V[f"D_cT_{tt}"])
                st.inc(vector.tensor_copy(cT_sb[:, tt * 128:(tt + 1) * 128],
                                          BANK(7, 128)[0:E + 1, :]), "cv")
                st.bset(7, "cv", st.cnt("cv"))
        V["D_DONE"] = st.cnt("cv")

        if upto == "D":
            _finish(nc, block, st, ap_dbg, dump, locals(), ap_out, xring)
            s_d.close(); s_c2.close(); s_cd.close(); s_bc.close()
            return nc

        s_d.close()
        s_cd.close()
        s_bc.close()
        w1r = [sb(f"f_w1_{i}", [128, NKC * 2048], BF16) for i in range(2)]
        w2r = [sb(f"f_w2_{i}", [128, 16 * D], BF16) for i in range(2)]
        h_sb = sb("f_h", [128, 16 * 256], BF16)
        ytr = [sb(f"f_yt_{i}", [128, 512]) for i in range(2)]
        x1h = [sb(f"g_x1_{i}", [128, 512]) for i in range(2)]

        def emit_wload(pi):
            ex, hf = pi // 2, pi % 2
            slot = pi % 2
            wsem = f"ws{slot}"

            @block.gpsimd
            def _(gpsimd):
                if pi >= 2:
                    st.wait(gpsimd, "ct", V[f"E_done_{pi - 2}"])
                else:
                    st.wait(gpsimd, "cv", V["D_DONE"])
                    for sx in range(3):
                        st.wait(gpsimd, f"sp{sx}", st.cnt(f"sp{sx}"))
                st.inc(gpsimd.dma_start(
                    out=w1r[slot][:].rearrange("p (kc n) -> p kc n", n=2048),
                    in_=ap_w1[ex].rearrange("(kc p) n -> p kc n", p=128)[:, :, hf * 2048:(hf + 1) * 2048]),
                    wsem, 16)
                st.inc(gpsimd.dma_start(
                    out=w2r[slot][:].rearrange("p (fc n) -> p fc n", n=D),
                    in_=ap_w2[ex].rearrange("(fc p) n -> p fc n", p=128)[:, hf * 16:(hf + 1) * 16, :]),
                    wsem, 16)
            V[f"E_wld_{pi}"] = st.cnt(wsem)

        emit_wload(0)
        emit_wload(1)

        for tt in range(NTT):
            for db in range(2):
                yslot = (2 * tt + db) % 2
                b = 4 + (2 * tt + db) % 3
                @block.tensor
                def _(tensor, tt=tt, db=db, b=b):
                    if tt == 0 and db == 0:
                        st.wait(tensor, "cv", V["D_DONE"])
                    st.bwait(tensor, b)
                    st.inc(tensor.matmul(BANK(b), cT_sb[:, tt * 128:(tt + 1) * 128],
                                         b2aug_sb[:, db * 512:(db + 1) * 512],
                                         start=True, stop=True), "ct")
                V[f"E_b2_{tt}_{db}"] = st.cnt("ct")

                @block.vector
                def _(vector, tt=tt, db=db, b=b, yslot=yslot):
                    st.wait(vector, "ct", V[f"E_b2_{tt}_{db}"])
                    g = st.bankg.get(200 + yslot)
                    if g is not None:
                        st.wait(vector, g[0], g[1])
                    st.inc(vector.tensor_copy(ytr[yslot][:, :], BANK(b)), "cv")
                    st.bset(b, "cv", st.cnt("cv"))
                V[f"E_b2ev_{tt}_{db}"] = st.cnt("cv")

                @block.gpsimd
                def _(gpsimd, tt=tt, db=db, yslot=yslot):
                    st.wait(gpsimd, "cv", V[f"E_b2ev_{tt}_{db}"])
                    st.inc(gpsimd.dma_start(
                        out=ap_facc[tt * 128:(tt + 1) * 128, db * 512:(db + 1) * 512],
                        in_=ytr[yslot][:, :]), f"fa{yslot}", 16)
                st.bset(200 + yslot, f"fa{yslot}", st.cnt(f"fa{yslot}"))

        yslot_i = 0
        NTB2 = 8
        hpair_guard = {}

        for pi in range(18):
            ex, hf = pi // 2, pi % 2
            slot = pi % 2
            wsem = f"ws{slot}"
            cs0 = st.cnt("cs")
            st.c["cs"] = cs0 + 8 * NTB2
            GEL = lambda tb, fp: cs0 + tb * 8 + fp + 1
            for tb in range(NTB2):
                for fp in range(8):
                    ps_ = fp % 2
                    @block.tensor
                    def _(tensor, pi=pi, tb=tb, fp=fp, ps_=ps_, slot=slot,
                          wsem=wsem, cs0=cs0):
                        if tb == 0 and fp == 0:
                            st.wait(tensor, wsem, V[f"E_wld_{pi}"])
                        gi = tb * 8 + fp - 2
                        if gi >= 0:
                            tensor.wait_ge(st.sems["cs"], cs0 + gi + 1)
                        elif hpair_guard.get(ps_) is not None:
                            tensor.wait_ge(st.sems["cs"], hpair_guard[ps_])
                        for fc in (2 * fp, 2 * fp + 1):
                            b = 2 * ps_ + (fc % 2)
                            for kc in range(NKC):
                                mm = tensor.matmul(
                                    BANK(b, 256),
                                    w1r[slot][:, kc * 2048 + fc * 128: kc * 2048 + (fc + 1) * 128],
                                    xfm[:, kc * S + tb * 256: kc * S + (tb + 1) * 256],
                                    start=(kc == 0), stop=(kc == NKC - 1))
                        st.inc(mm, "ct")
                        V[f"E_hmm_{pi}_{tb}_{fp}"] = st.cnt("ct")

                for tt2 in range(2):
                    tt_g = tb * 2 + tt2
                    for db in range(2):
                        b = 4 + (tt2 * 2 + db) % 3
                        @block.tensor
                        def _(tensor, pi=pi, tb=tb, tt2=tt2, db=db, b=b,
                              slot=slot, cs0=cs0):
                            if tt2 == 0 and db == 0:
                                st.wait(tensor, "cs", GEL(tb, 7))
                            st.bwait(tensor, b)
                            for fc in range(16):
                                mm = tensor.matmul(
                                    BANK(b),
                                    h_sb[:, fc * 256 + tt2 * 128: fc * 256 + tt2 * 128 + 128],
                                    w2r[slot][:, fc * D + db * 512: fc * D + db * 512 + 512],
                                    start=(fc == 0), stop=(fc == 15))
                            st.inc(mm, "ct")
                        V[f"E_ymm_{pi}_{tb}_{tt2}_{db}"] = st.cnt("ct")

                        yslot = yslot_i % 2
                        yslot_i += 1
                        @block.vector
                        def _(vector, pi=pi, ex=ex, tb=tb, tt2=tt2, db=db, b=b,
                              yslot=yslot, tt_g=tt_g):
                            st.wait(vector, "ct", V[f"E_ymm_{pi}_{tb}_{tt2}_{db}"])
                            g = st.bankg.get(200 + yslot)
                            if g is not None:
                                st.wait(vector, g[0], g[1])
                            if ex < E:
                                op = vector.tensor_scalar(
                                    out=ytr[yslot][:, :], in0=BANK(b),
                                    scalar1=comb_sb[:, tt_g * (E + 1) + ex: tt_g * (E + 1) + ex + 1],
                                    scalar2=None, op0=ALU.mult)
                            else:
                                op = vector.tensor_copy(ytr[yslot][:, :], BANK(b))
                            st.inc(op, "cv")
                            st.bset(b, "cv", st.cnt("cv"))
                        V[f"E_yev_{pi}_{tb}_{tt2}_{db}"] = st.cnt("cv")

                        @block.gpsimd
                        def _(gpsimd, pi=pi, tb=tb, tt2=tt2, db=db, yslot=yslot,
                              tt_g=tt_g):
                            st.wait(gpsimd, "cv", V[f"E_yev_{pi}_{tb}_{tt2}_{db}"])
                            st.inc(gpsimd.dma_start(
                                out=ap_facc[tt_g * 128:(tt_g + 1) * 128,
                                            db * 512:(db + 1) * 512],
                                in_=ytr[yslot][:, :], accum_op=ALU.add),
                                f"fa{yslot}", 16)
                        st.bset(200 + yslot, f"fa{yslot}", st.cnt(f"fa{yslot}"))

            @block.scalar
            def _(scalar, pi=pi, ex=ex, hf=hf, cs0=cs0):
                for tb in range(NTB2):
                    for fp in range(8):
                        ps_ = fp % 2
                        scalar.wait_ge(st.sems["ct"], V[f"E_hmm_{pi}_{tb}_{fp}"])
                        scalar.activation(
                            out=h_sb[:, fp * 512:(fp + 1) * 512].rearrange(
                                "p (b n) -> p b n", n=256),
                            in_=psum[:, ps_ * 1024: ps_ * 1024 + 1024].rearrange(
                                "p (b n) -> p b n", n=512)[:, :, 0:256],
                            func=AF.Gelu,
                            bias=b1_sb[:, ex * 32 + hf * 16 + 2 * fp: ex * 32 + hf * 16 + 2 * fp + 1],
                            scale=1.0).then_inc(st.sems["cs"], 1)
            hpair_guard[0] = cs0 + 8 * NTB2 - 1
            hpair_guard[1] = cs0 + 8 * NTB2

            V[f"E_done_{pi}"] = st.cnt("ct")
            if pi + 2 < 18:
                emit_wload(pi + 2)

        if upto == "E":
            _finish(nc, block, st, ap_dbg, dump, locals(), ap_out, xring)
            return nc

        fring = [xring[0], xring[1]]
        o_t = [lring[0], lring[1]]
        x1p = [[x1h[0], x1h[1]], [ytr[0], ytr[1]]]
        FA_TOT = {x: st.cnt(f"fa{x}") for x in range(2)}
        SP_TOT = {x: st.cnt(f"sp{x}") for x in range(3)}

        def f_head(tt):
            fs = f"lf{tt % 2}"
            xs = f"lx{tt % 2}"

            @block.sync
            def _(sync, tt=tt, fs=fs, xs=xs):
                if tt == 0:
                    for x in range(2):
                        st.wait(sync, f"fa{x}", FA_TOT[x])
                    for x in range(3):
                        st.wait(sync, f"sp{x}", SP_TOT[x])
                if tt >= 2:
                    st.wait(sync, "cv", V[f"F_o_{tt - 2}"])
                st.inc(sync.dma_start(out=fring[tt % 2][:],
                                      in_=ap_facc[tt * 128:(tt + 1) * 128, :]), fs, 16)
                st.inc(sync.dma_start(out=x1p[tt % 2][0][:, :],
                                      in_=ap_x1s[tt * 128:(tt + 1) * 128, 0:512]), xs, 16)
                st.inc(sync.dma_start(out=x1p[tt % 2][1][:, :],
                                      in_=ap_x1s[tt * 128:(tt + 1) * 128, 512:1024]), xs, 16)
            V[f"F_lf_{tt}"] = st.cnt(fs)
            V[f"F_lx_{tt}"] = st.cnt(xs)

            def pref(vector, tt=tt, fs=fs):
                st.wait(vector, fs, V[f"F_lf_{tt}"])
            ln_stats(tt % 2, fring[tt % 2][:, :], f"F_{tt}", pref)
            ln_rstd(tt % 2, f"F_{tt}")

        def f_tail(u):
            pu = u % 2
            L = lnS[pu]

            @block.vector
            def _(vector, u=u, pu=pu, L=L):
                st.wait(vector, "cs", V[f"F_{u}_rstd"])
                st.wait(vector, f"lx{pu}", V[f"F_lx_{u}"])
                if u >= 2:
                    st.wait(vector, f"so{u % 2}", V[f"F_st_{u - 2}"])
                vector.tensor_scalar(out=o_t[pu][:, :], in0=fring[pu][:, :],
                                     scalar1=L.tmp2[:, 0:1], scalar2=L.sr[:, :],
                                     op0=ALU.subtract, op1=ALU.mult)
                vector.drain()
                vector.tensor_tensor(o_t[pu][:, 0:512], o_t[pu][:, 0:512],
                                     x1p[pu][0][:, :], ALU.add)
                st.inc(vector.tensor_tensor(o_t[pu][:, 512:1024],
                                            o_t[pu][:, 512:1024],
                                            x1p[pu][1][:, :], ALU.add), "cv")
            V[f"F_o_{u}"] = st.cnt("cv")

            @block.sync
            def _(sync, u=u):
                st.wait(sync, "cv", V[f"F_o_{u}"])
                st.inc(sync.dma_start(out=ap_out[u * 128:(u + 1) * 128, :],
                                      in_=o_t[u % 2][:, :]), f"so{u % 2}", 16)
            V[f"F_st_{u}"] = st.cnt(f"so{u % 2}")

        for tt in range(NTT + 1):
            if tt < NTT:
                f_head(tt)
            if tt >= 1:
                f_tail(tt - 1)

        @block.sync
        def _(sync):
            sync.wait_ge(st.sems["so0"], st.cnt("so0"))
            sync.wait_ge(st.sems["so1"], st.cnt("so1"))

    return nc


def _finish(nc, block, st, ap_dbg, dump, scope, ap_out, xring):
    @block.sync
    def _(sync):
        for k in ["cv", "cs", "ct"]:
            if st.cnt(k):
                sync.wait_ge(st.sems[k], st.cnt(k))
        n = 0
        if ap_dbg is not None:
            name = dump[0]
            buf = scope[name]
            rows = dump[1][0]
            st.inc(sync.dma_start(out=ap_dbg[:, :], in_=buf[0:rows, 0:dump[1][1]]),
                   "st_o", 16)
            n += 1
        for tt in range(NTT):
            st.inc(sync.dma_start(out=ap_out[tt * 128:(tt + 1) * 128, :],
                                  in_=xring[tt % 2][:, :]), "st_o", 16)
        sync.wait_ge(st.sems["st_o"], st.cnt("st_o"))


def make_inputs_per_core(inputs):
    cos2, ssin2 = _host_tables()
    masks = _host_masks()
    eye = np.eye(128, dtype=np.float32)
    cvals = np.zeros((128, 8), dtype=np.float32)
    cvals[:, 0] = EPS
    cvals[:, 1] = 0.125
    cvals[:, 2] = -10.0
    cvals[:, 3] = 1.0
    cvals[:, 4] = 0.0
    cvals[:, 5] = -0.5

    w1 = np.concatenate([np.asarray(inputs["L0_ffn_W1"]),
                         np.asarray(inputs["L0_ffn_sW1"])[None]], axis=0)
    w2 = np.concatenate([np.asarray(inputs["L0_ffn_W2"]),
                         np.asarray(inputs["L0_ffn_sW2"])[None]], axis=0)
    b1 = np.concatenate([np.asarray(inputs["L0_ffn_b1"]),
                         np.asarray(inputs["L0_ffn_sb1"])[None]], axis=0)
    b1c = np.ascontiguousarray(b1.reshape(E + 1, 32, 128).transpose(2, 0, 1)
                               .reshape(128, (E + 1) * 32))
    b2aug = np.ascontiguousarray(
        np.concatenate([np.asarray(inputs["L0_ffn_b2"]),
                        np.asarray(inputs["L0_ffn_sb2"])[None]], axis=0))
    gwT = np.ascontiguousarray(np.asarray(inputs["L0_ffn_gate_w"]).T)

    shared = {
        "Wq": np.ascontiguousarray(np.asarray(inputs["L0_attn_Wq"])),
        "Wk": np.ascontiguousarray(np.asarray(inputs["L0_attn_Wk"])),
        "Wv": np.ascontiguousarray(np.asarray(inputs["L0_attn_Wv"])),
        "Wo": np.ascontiguousarray(np.asarray(inputs["L0_attn_Wo"])),
        "gwT": gwT, "W1": np.ascontiguousarray(w1), "W2": np.ascontiguousarray(w2),
        "b1c": b1c, "b2aug": b2aug, "costab": cos2, "ssintab": ssin2,
        "masks": masks, "eye": eye, "consts": cvals,
    }
    x = np.asarray(inputs["L0_attn_x"])
    in_maps = []
    for c in range(8):
        m = dict(shared)
        m["x"] = np.ascontiguousarray(x[c])
        in_maps.append(m)
    return in_maps


def kernel(**inputs):
    in_maps = make_inputs_per_core(inputs)
    nc = build_program("full")
    res = run_bass_kernel_spmd(nc, in_maps, core_ids=list(range(8)))
    out = np.stack([np.asarray(res.results[c]["out"]) for c in range(8)], axis=0)
    return out.astype(np.float32)


# version 09
# speedup vs baseline: 1.0189x; 1.0189x over previous
import numpy as np
import ml_dtypes
from contextlib import ExitStack

import concourse.bass as bass
import concourse.mybir as mybir
from concourse.bass_utils import run_bass_kernel_spmd

F32 = mybir.dt.float32
BF16 = mybir.dt.bfloat16
AF = mybir.ActivationFunctionType
ALU = mybir.AluOpType
AX = mybir.AxisListType.X

S, D = 2048, 1024
NH, KVH, HD = 16, 4, 64
E, FF = 8, 4096
NTT, NTB, NKC = 16, 4, 8
EPS = 1e-5


def _host_tables():
    half = HD // 2
    try:
        import jax
        cpu = jax.devices("cpu")[0]
        import jax.numpy as jnp
        with jax.default_device(cpu):
            inv_freq = 1.0 / 10000.0 ** (jnp.arange(half, dtype=jnp.float32) / half)
            t = jnp.arange(S, dtype=jnp.float32)
            freqs = jnp.einsum("i,j->ij", t, inv_freq)
            emb = jnp.concatenate((freqs, freqs), axis=-1)
            cos = np.asarray(jnp.cos(emb), dtype=np.float32)
            sin = np.asarray(jnp.sin(emb), dtype=np.float32)
    except Exception:
        inv_freq = (1.0 / (np.float32(10000.0) **
                           (np.arange(half, dtype=np.float32) / np.float32(half)))).astype(np.float32)
        t = np.arange(S, dtype=np.float32)
        freqs = (t[:, None] * inv_freq[None, :]).astype(np.float32)
        emb = np.concatenate((freqs, freqs), axis=-1)
        cos = np.cos(emb).astype(np.float32)
        sin = np.sin(emb).astype(np.float32)
    cos_fm = cos.T.copy()
    sin_fm = sin.T.copy()
    ssin = sin_fm.copy()
    ssin[:32] = -sin_fm[:32]
    return (np.ascontiguousarray(np.concatenate([cos_fm, cos_fm], axis=0)),
            np.ascontiguousarray(np.concatenate([ssin, ssin], axis=0)))


def _host_masks():
    p = np.arange(128)[:, None]
    j = np.arange(512)[None, :]
    ms = [(128 * r + p <= j).astype(np.float32) for r in range(4)]
    return np.ascontiguousarray(np.concatenate(ms, axis=1).astype(ml_dtypes.bfloat16))


class St:
    def __init__(self):
        self.c = {}
        self.sems = {}
        self.bankg = {}

    def inc(self, instr, key, n=1):
        instr.then_inc(self.sems[key], n)
        self.c[key] = self.c.get(key, 0) + n
        return self.c[key]

    def cnt(self, key):
        return self.c.get(key, 0)

    def wait(self, eng, key, val):
        if val and val > 0:
            eng.wait_ge(self.sems[key], val)

    def bwait(self, eng, b):
        g = self.bankg.get(b)
        if g is not None:
            self.wait(eng, g[0], g[1])

    def bset(self, b, key, val):
        self.bankg[b] = (key, val)


def build_program(upto="full", dump=None):
    nc = bass.Bass("TRN2", target_bir_lowering=False)
    ap_x = nc.dram_tensor("x", [S, D], F32, kind="ExternalInput").ap()
    ap_wq = nc.dram_tensor("Wq", [D, D], F32, kind="ExternalInput").ap()
    ap_wk = nc.dram_tensor("Wk", [D, 256], F32, kind="ExternalInput").ap()
    ap_wv = nc.dram_tensor("Wv", [D, 256], F32, kind="ExternalInput").ap()
    ap_wo = nc.dram_tensor("Wo", [D, D], F32, kind="ExternalInput").ap()
    ap_gwT = nc.dram_tensor("gwT", [D, E], F32, kind="ExternalInput").ap()
    ap_w1 = nc.dram_tensor("W1", [E + 1, D, FF], F32, kind="ExternalInput").ap()
    ap_w2 = nc.dram_tensor("W2", [E + 1, FF, D], F32, kind="ExternalInput").ap()
    ap_b1 = nc.dram_tensor("b1c", [128, (E + 1) * 32], F32, kind="ExternalInput").ap()
    ap_b2aug = nc.dram_tensor("b2aug", [E + 1, D], F32, kind="ExternalInput").ap()
    ap_cos = nc.dram_tensor("costab", [128, S], F32, kind="ExternalInput").ap()
    ap_ssin = nc.dram_tensor("ssintab", [128, S], F32, kind="ExternalInput").ap()
    ap_masks = nc.dram_tensor("masks", [128, 2048], BF16, kind="ExternalInput").ap()
    ap_eye = nc.dram_tensor("eye", [128, 128], F32, kind="ExternalInput").ap()
    ap_consts = nc.dram_tensor("consts", [128, 8], F32, kind="ExternalInput").ap()
    ap_ind2 = nc.dram_tensor("ind2", [2, 128], F32, kind="ExternalInput").ap()
    ap_out = nc.dram_tensor("out", [S, D], F32, kind="ExternalOutput").ap()
    ap_x1s = nc.dram_tensor("x1scr", [S, D], F32, kind="Internal").ap()
    ap_facc = nc.dram_tensor("ffnacc", [S, D], F32, kind="Internal").ap()
    ap_dbg = None
    if dump is not None:
        ap_dbg = nc.dram_tensor("dbg", list(dump[1]), dump[2], kind="ExternalOutput").ap()

    st = St()
    V = st.c

    with ExitStack() as stack:
        en = stack.enter_context
        sb = lambda n, sh, dt=F32: en(nc.sbuf_tensor(n, sh, dt))

        consts = sb("c_consts", [128, 8])
        eye = sb("c_eye", [128, 128])
        b1_sb = sb("c_b1", [128, (E + 1) * 32])
        b2aug_sb = sb("c_b2aug", [E + 1, D])
        gwT_sb = sb("c_gwT", [128, NKC * E])
        ind2_sb = sb("c_ind2", [2, 128])
        xfm = sb("p_xfm", [128, NKC * S], BF16)
        comb_sb = sb("p_comb", [128, NTT * (E + 1)])
        cT_sb = sb("p_cT", [E + 1, S])
        class _LnScr:
            def __init__(self, i):
                self.st6 = sb(f"ln{i}_st6", [128, 12])
                self.st2a = sb(f"ln{i}_st2a", [128, 2])
                self.st2b = sb(f"ln{i}_st2b", [128, 2])
                self.tmp2 = sb(f"ln{i}_tmp2", [128, 2])
                self.d1 = sb(f"ln{i}_d1", [128, 1])
                self.var1 = sb(f"ln{i}_var1", [128, 1])
                self.sstd = sb(f"ln{i}_sstd", [128, 1])
                self.sr = sb(f"ln{i}_sr", [128, 1])
        lnS = [_LnScr(0), _LnScr(1)]
        xring = [sb(f"r_x{i}", [128, D]) for i in range(2)]
        lring = [sb(f"r_ln{i}", [128, D]) for i in range(2)]

        s_bc = ExitStack()
        bcb = lambda n, sh, dt=F32: s_bc.enter_context(nc.sbuf_tensor(n, sh, dt))
        wo_sb = bcb("a_wo", [128, NKC * D], BF16)
        q_sb = bcb("a_q", [128, 8 * S], BF16)
        k2_sb = bcb("a_k2", [128, KVH * S], BF16)
        v_sb = bcb("a_v", [128, NTT * KVH * 65], BF16)

        s_b = ExitStack()
        bb = lambda n, sh, dt=F32: s_b.enter_context(nc.sbuf_tensor(n, sh, dt))
        costab = bb("a_cos", [128, S])
        ssintab = bb("a_ssin", [128, S])
        wq_sb = bb("a_wq", [128, NKC * D], BF16)
        wk_sb = bb("a_wk", [128, NKC * 256], BF16)
        wv_sb = bb("a_wv", [128, NKC * 256], BF16)
        qraw = [bb(f"r_qraw{i}", [128, S]) for i in range(2)]
        qswp = [bb(f"r_qswp{i}", [128, S]) for i in range(2)]

        psum = en(nc.psum_tensor("psum", [128, 4096], F32))
        BANK = lambda b, w=512: psum[:, b * 512: b * 512 + w]

        for name in ["ld0", "ld1", "ld2", "ld3", "st_o", "ws0", "ws1", "cst",
                     "swp0", "swp1", "sp0", "sp1", "sp2", "fa0", "fa1", "fa2", "fa3",
                     "lf0", "lf1", "lx0", "lx1", "so0", "so1", "cv", "cs", "ct", "cg"]:
            st.sems[name] = en(nc.semaphore(name))

        nc.const_aps.register(F32, EPS, consts[:, 0:1])
        nc.const_aps.register(F32, 0.125, consts[:, 1:2])
        nc.const_aps.register(F32, -10.0, consts[:, 2:3])
        nc.const_aps.register(F32, 1.0, consts[:, 3:4])
        nc.const_aps.register(F32, 0.0, consts[:, 4:5])
        nc.const_aps.register(F32, -0.5, consts[:, 5:6])

        block = en(nc.Block())

        @block.scalar
        def _(scalar):
            st.inc(scalar.dma_start(out=consts[:], in_=ap_consts[:, :]), "cst", 16)
            st.inc(scalar.dma_start(out=ind2_sb[:], in_=ap_ind2[:, :]), "cst", 16)
            st.inc(scalar.dma_start(out=eye[:], in_=ap_eye[:, :]), "cst", 16)
            st.inc(scalar.dma_start(out=costab[:], in_=ap_cos[:, :]), "cst", 16)
            st.inc(scalar.dma_start(out=ssintab[:], in_=ap_ssin[:, :]), "cst", 16)
            st.inc(scalar.dma_start(out=b1_sb[:], in_=ap_b1[:, :]), "cst", 16)
            st.inc(scalar.dma_start(out=b2aug_sb[:], in_=ap_b2aug[:, :]), "cst", 16)
            st.inc(scalar.dma_start(out=gwT_sb[:].rearrange("p (kc n) -> p kc n", n=E),
                                    in_=ap_gwT.rearrange("(kc p) n -> p kc n", p=128)),
                   "cst", 16)
        V["CST_ALL"] = st.cnt("cst")

        @block.gpsimd
        def _(gpsimd):

            gpsimd.memset(
                comb_sb[:].rearrange("p (tt e) -> p tt e", e=E + 1)[:, :, E:E + 1], 1.0)
            st.inc(gpsimd.memset(
                v_sb[:].rearrange("p (tt k d) -> p tt k d", k=KVH, d=65)[:, :, :, 64:65],
                1.0), "cg", 1)
            st.inc(gpsimd.dma_start(
                out=wq_sb[:].rearrange("p (kc n) -> p kc n", n=D),
                in_=ap_wq.rearrange("(kc p) n -> p kc n", p=128)), "ws0", 16)
            st.inc(gpsimd.dma_start(
                out=wk_sb[:].rearrange("p (kc n) -> p kc n", n=256),
                in_=ap_wk.rearrange("(kc p) n -> p kc n", p=128)), "ws0", 16)
            st.inc(gpsimd.dma_start(
                out=wv_sb[:].rearrange("p (kc n) -> p kc n", n=256),
                in_=ap_wv.rearrange("(kc p) n -> p kc n", p=128)), "ws0", 16)
            st.inc(gpsimd.dma_start(
                out=wo_sb[:].rearrange("p (kc n) -> p kc n", n=D),
                in_=ap_wo.rearrange("(kc p) n -> p kc n", p=128)), "ws0", 16)
        V["CG_INIT"] = st.cnt("cg")
        V["WS_QKVO"] = st.cnt("ws0")

        @block.scalar
        def _(scalar):
            st.wait(scalar, "cst", V["CST_ALL"])

        ln_guard = {}

        def ln_stats(p_, tile_ap, tag, pre=None):
            L = lnS[p_]

            @block.vector
            def _(vector):
                g = ln_guard.get(p_)
                if g is not None:
                    st.wait(vector, "cs", g)
                if pre is not None:
                    pre(vector)
                vector.drain()
                vector.bn_stats(L.st6[:, 0:6], tile_ap[:, 0:512])
                vector.bn_stats(L.st6[:, 6:12], tile_ap[:, 512:1024])
                vector.drain()
                vector.bn_aggr(L.st2a[:, :], L.st6[:, 0:6])
                vector.bn_aggr(L.st2b[:, :], L.st6[:, 6:12])
                vector.drain()
                vector.tensor_tensor(L.tmp2[:, :], L.st2a[:, :], L.st2b[:, :], ALU.add)
                vector.tensor_tensor(L.d1[:, :], L.st2a[:, 0:1], L.st2b[:, 0:1],
                                     ALU.subtract)
                vector.drain()
                vector.tensor_scalar(out=L.tmp2[:, :], in0=L.tmp2[:, :], scalar1=0.5,
                                     scalar2=None, op0=ALU.mult)
                vector.tensor_tensor(L.d1[:, :], L.d1[:, :], L.d1[:, :], ALU.mult)
                vector.drain()
                st.inc(vector.scalar_tensor_tensor(
                    out=L.var1[:, :], in0=L.d1[:, :], scalar=0.25,
                    in1=L.tmp2[:, 1:2], op0=ALU.mult, op1=ALU.add), "cv")
            V[f"{tag}_var"] = st.cnt("cv")

        def ln_rstd(p_, tag):
            L = lnS[p_]

            @block.scalar
            def _(scalar):
                st.wait(scalar, "cv", V[f"{tag}_var"])
                scalar.activation(out=L.sstd[:, :], in_=L.var1[:, :],
                                  func=AF.Ln, bias=EPS, scale=1.0)
                scalar.drain()
                st.inc(scalar.activation(out=L.sr[:, :], in_=L.sstd[:, :],
                                         func=AF.Exp, bias=0.0, scale=-0.5), "cs")
            V[f"{tag}_rstd"] = st.cnt("cs")
            ln_guard[p_] = st.cnt("cs")

        def ln_norm(p_, tile_ap, out_ap, tag, pre=None):
            L = lnS[p_]

            @block.vector
            def _(vector):
                st.wait(vector, "cs", V[f"{tag}_rstd"])
                if pre is not None:
                    pre(vector)
                st.inc(vector.tensor_scalar(
                    out=out_ap, in0=tile_ap, scalar1=L.tmp2[:, 0:1],
                    scalar2=L.sr[:, :], op0=ALU.subtract, op1=ALU.mult), "cv")
            V[f"{tag}_ln"] = st.cnt("cv")

        def a_tail(u):
            def pre(vector, u=u):
                if u >= 2:
                    st.wait(vector, "ct", V[f"A_tp_{u - 2}"])
            ln_norm(u % 2, xring[u % 2][:, :], lring[u % 2][:, :], f"A{u}", pre)
            for kc in range(NKC):
                @block.tensor
                def _(tensor, u=u, kc=kc):
                    if kc == 0:
                        st.wait(tensor, "cv", V[f"A{u}_ln"])
                    b = kc % 4
                    st.bwait(tensor, b)
                    st.inc(tensor.transpose(BANK(b, 128),
                                            lring[u % 2][:, kc * 128:(kc + 1) * 128],
                                            eye[:, :]), "ct")
                    V[f"A_tp_{u}_{kc}"] = st.cnt("ct")

                @block.vector
                def _(vector, u=u, kc=kc):
                    st.wait(vector, "ct", V[f"A_tp_{u}_{kc}"])
                    cp = vector.tensor_copy(
                        xfm[:, kc * S + u * 128: kc * S + (u + 1) * 128],
                        BANK(kc % 4, 128))
                    st.inc(cp, "cv")
                    st.bset(kc % 4, "cv", st.cnt("cv"))
            V[f"A_tp_{u}"] = st.cnt("ct")

        for tt in range(NTT + 1):
            if tt < NTT:
                sem = f"ld{tt % 2}"

                @block.sync
                def _(sync, tt=tt, sem=sem):
                    if tt >= 2:
                        st.wait(sync, "cv", V[f"A{tt - 2}_ln"])
                    st.inc(sync.dma_start(out=xring[tt % 2][:],
                                          in_=ap_x[tt * 128:(tt + 1) * 128, :]),
                           sem, 16)
                V[f"A_ld_{tt}"] = st.cnt(sem)

                def pre(vector, tt=tt, sem=sem):
                    st.wait(vector, sem, V[f"A_ld_{tt}"])
                ln_stats(tt % 2, xring[tt % 2][:, :], f"A{tt}", pre)
                ln_rstd(tt % 2, f"A{tt}")
            if tt >= 1:
                a_tail(tt - 1)
        V["A_DONE"] = st.cnt("cv")

        if upto == "A":
            _finish(nc, block, st, ap_dbg, dump, locals(), ap_out, xring)
            s_b.close(); s_bc.close()
            return nc

        def emit_rope(c):
            is_k = c >= 8
            sl = c % 2

            @block.vector
            def _(vector, c=c, is_k=is_k, sl=sl):
                st.wait(vector, f"swp{c % 2}", V[f"B_sw_{c}"])
                if c == 0:
                    st.wait(vector, "cst", V["CST_ALL"])
                vector.drain()
                vector.tensor_tensor(qraw[sl][:, :], qraw[sl][:, :], costab[:, :],
                                     ALU.mult)
                vector.tensor_tensor(qswp[sl][:, :], qswp[sl][:, :], ssintab[:, :],
                                     ALU.mult)
                vector.drain()
                if not is_k:
                    st.inc(vector.tensor_tensor(q_sb[:, c * S:(c + 1) * S],
                                                qraw[sl][:, :], qswp[sl][:, :],
                                                ALU.add), "cv")
                else:
                    for half, kvh in ((0, 2 * (c - 8)), (1, 2 * (c - 8) + 1)):
                        lo = half * 64
                        vector.tensor_tensor(k2_sb[0:64, kvh * S:(kvh + 1) * S],
                                             qraw[sl][lo:lo + 64, :],
                                             qswp[sl][lo:lo + 64, :], ALU.add)
                        cp = vector.tensor_tensor(k2_sb[64:128, kvh * S:(kvh + 1) * S],
                                                  qraw[sl][lo:lo + 64, :],
                                                  qswp[sl][lo:lo + 64, :], ALU.add)
                    st.inc(cp, "cv")
            V[f"B_rope_{c}"] = st.cnt("cv")

        for c in range(10):
            is_k = c >= 8
            sl = c % 2
            @block.tensor
            def _(tensor, c=c, is_k=is_k):
                if c == 0:
                    st.wait(tensor, "ws0", V["WS_QKVO"])
                    st.wait(tensor, "cv", V["A_DONE"])
                for tb in range(NTB):
                    b = 4 + tb
                    st.bwait(tensor, b)
                    for kc in range(NKC):
                        if is_k:
                            lhs = wk_sb[:, kc * 256 + (c - 8) * 128: kc * 256 + (c - 7) * 128]
                        else:
                            lhs = wq_sb[:, kc * D + c * 128: kc * D + (c + 1) * 128]
                        mm = tensor.matmul(BANK(b),
                                           lhs,
                                           xfm[:, kc * S + tb * 512: kc * S + (tb + 1) * 512],
                                           start=(kc == 0), stop=(kc == NKC - 1))
                    st.inc(mm, "ct")
                    V[f"B_mm_{c}_{tb}"] = st.cnt("ct")

            @block.vector
            def _(vector, c=c, sl=sl):
                vector.drain()
                for tb in range(NTB):
                    st.wait(vector, "ct", V[f"B_mm_{c}_{tb}"])
                    cp = vector.tensor_copy(qraw[sl][:, tb * 512:(tb + 1) * 512],
                                            BANK(4 + tb))
                    st.inc(cp, "cv")
                    st.bset(4 + tb, "cv", st.cnt("cv"))
            V[f"B_ev_{c}"] = st.cnt("cv")

            @block.scalar
            def _(scalar, c=c, sl=sl):
                st.wait(scalar, "cv", V[f"B_ev_{c}"])
                sw = st.sems[f"swp{c % 2}"]
                scalar.dma_start(out=qswp[sl][0:32], in_=qraw[sl][32:64]).then_inc(sw, 16)
                scalar.dma_start(out=qswp[sl][32:64], in_=qraw[sl][0:32]).then_inc(sw, 16)
                scalar.dma_start(out=qswp[sl][64:96], in_=qraw[sl][96:128]).then_inc(sw, 16)
                scalar.dma_start(out=qswp[sl][96:128], in_=qraw[sl][64:96]).then_inc(sw, 16)
            st.c[f"swp{c % 2}"] = st.cnt(f"swp{c % 2}") + 64
            V[f"B_sw_{c}"] = st.cnt(f"swp{c % 2}")

            if c >= 1:
                emit_rope(c - 1)
        emit_rope(9)

        for tt in range(NTT):
            @block.tensor
            def _(tensor, tt=tt):
                b = 4 + tt % 4
                st.bwait(tensor, b)
                for kc in range(NKC):
                    mm = tensor.matmul(BANK(b, 256),
                                       xfm[:, kc * S + tt * 128: kc * S + (tt + 1) * 128],
                                       wv_sb[:, kc * 256:(kc + 1) * 256],
                                       start=(kc == 0), stop=(kc == NKC - 1))
                st.inc(mm, "ct")
                V[f"B_vmm_{tt}"] = st.cnt("ct")

            @block.vector
            def _(vector, tt=tt):
                st.wait(vector, "ct", V[f"B_vmm_{tt}"])
                if tt == 0:
                    st.wait(vector, "cg", V["CG_INIT"])
                for kvh in range(KVH):
                    cp = vector.tensor_copy(
                        v_sb[:, tt * KVH * 65 + kvh * 65: tt * KVH * 65 + kvh * 65 + 64],
                        BANK(4 + tt % 4, 256)[:, kvh * 64:(kvh + 1) * 64])
                st.inc(cp, "cv")
                st.bset(4 + tt % 4, "cv", st.cnt("cv"))
        V["B_DONE"] = st.cnt("cv")

        if upto == "B":
            _finish(nc, block, st, ap_dbg, dump, locals(), ap_out, xring)
            s_b.close(); s_bc.close()
            return nc

        s_b.close()
        s_cd = ExitStack()
        yall = s_cd.enter_context(nc.sbuf_tensor("a_yall", [128, 8 * S], BF16))
        s_c2 = ExitStack()
        c2b = lambda n, sh, dt=F32: s_c2.enter_context(nc.sbuf_tensor(n, sh, dt))
        masks_sb = c2b("a_masks", [128, 2048], BF16)
        attw = c2b("a_attw", [128, 16 * 1024], BF16)
        zstr = lring[0]
        ytmp_att = lring[1]

        @block.scalar
        def _(scalar):
            st.inc(scalar.dma_start(out=masks_sb[:], in_=ap_masks[:, :]), "cst", 16)
        V["CST2"] = st.cnt("cst")

        sc_i = 0
        pending_zb = []
        pair_guard = {}
        for hp in range(8):
            kvh = hp // 2
            for qb in range(NTB):
                n_kc = 4 * (qb + 1)
                kcs = list(range(4 * qb, 4 * qb + 4)) + list(range(0, 4 * qb))
                ct0 = st.cnt("ct")
                cs0 = st.cnt("cs")
                st.c["ct"] = ct0 + n_kc
                st.c["cs"] = cs0 + n_kc
                for i, kc in enumerate(kcs):
                    V[f"C_sc_{hp}_{qb}_{kc}"] = ct0 + i + 1
                    V[f"C_exp_{hp}_{qb}_{kc}"] = cs0 + i + 1

                @block.tensor
                def _(tensor, hp=hp, qb=qb, kcs=kcs, kvh=kvh, ct0=ct0, cs0=cs0):
                    if hp == 0 and qb == 0:
                        st.wait(tensor, "cv", V["B_DONE"])
                        st.wait(tensor, "cst", V["CST_ALL"])
                    for i, kc in enumerate(kcs):
                        pair = (sc_i + i) % 2
                        b0 = 0 if pair == 0 else 2
                        if i >= 2:
                            tensor.wait_ge(st.sems["cs"], cs0 + i - 1)
                        elif pair_guard.get(pair) is not None:
                            tensor.wait_ge(st.sems["cs"], pair_guard[pair])
                        tensor.matmul(BANK(b0),
                                      k2_sb[0:64, kvh * S + kc * 128: kvh * S + (kc + 1) * 128],
                                      q_sb[0:64, hp * S + qb * 512: hp * S + (qb + 1) * 512],
                                      start=True, stop=True, tile_position=(0, 0))
                        tensor.matmul(
                            BANK(b0 + 1),
                            k2_sb[64:128, kvh * S + kc * 128: kvh * S + (kc + 1) * 128],
                            q_sb[64:128, hp * S + qb * 512: hp * S + (qb + 1) * 512],
                            start=True, stop=True,
                            tile_position=(64, 0)).then_inc(st.sems["ct"], 1)

                @block.scalar
                def _(scalar, hp=hp, qb=qb, kcs=kcs, ct0=ct0):
                    pv = V.get("C_prev_attv")
                    if pv is not None:
                        st.wait(scalar, "ct", pv)
                    for i, kc in enumerate(kcs):
                        pair = (sc_i + i) % 2
                        b0 = 0 if pair == 0 else 2
                        scalar.wait_ge(st.sems["ct"], ct0 + i + 1)
                        scalar.activation(
                            out=attw[:, kc * 1024:(kc + 1) * 1024],
                            in_=psum[:, b0 * 512: b0 * 512 + 1024],
                            func=AF.Exp, bias=-10.0,
                            scale=0.125).then_inc(st.sems["cs"], 1)

                pair_guard[(sc_i + n_kc - 2) % 2] = cs0 + n_kc - 1
                pair_guard[(sc_i + n_kc - 1) % 2] = cs0 + n_kc
                sc_i += n_kc

                @block.vector
                def _(vector, hp=hp, qb=qb):
                    st.wait(vector, "cst", V["CST2"])
                    st.wait(vector, "cs", V[f"C_exp_{hp}_{qb}_{4 * qb + 3}"])
                    for r in range(4):
                        kc = 4 * qb + r
                        for hh in range(2):
                            mk = vector.tensor_tensor(
                                attw[:, kc * 1024 + hh * 512: kc * 1024 + hh * 512 + 512],
                                attw[:, kc * 1024 + hh * 512: kc * 1024 + hh * 512 + 512],
                                masks_sb[:, r * 512:(r + 1) * 512], ALU.mult)
                    st.inc(mk, "cv")
                V[f"C_mask_{hp}_{qb}"] = st.cnt("cv")

                last_kc = kcs[-1]
                if pending_zb:
                    fn = pending_zb.pop(0)
                    fn()

                @block.tensor
                def _(tensor, hp=hp, qb=qb, n_kc=n_kc, last_kc=last_kc, kvh=kvh):
                    st.wait(tensor, "cv", V[f"C_mask_{hp}_{qb}"])
                    st.wait(tensor, "cs", V[f"C_exp_{hp}_{qb}_{last_kc}"])
                    st.bwait(tensor, 4)
                    st.bwait(tensor, 5)
                    for i_kc in range(n_kc):
                        for hh in range(2):
                            mm = tensor.matmul(
                                BANK(4 + hh)[0:65, :],
                                v_sb[:, i_kc * KVH * 65 + kvh * 65: i_kc * KVH * 65 + kvh * 65 + 65],
                                attw[:, i_kc * 1024 + hh * 512: i_kc * 1024 + hh * 512 + 512],
                                start=(i_kc == 0), stop=(i_kc == n_kc - 1))
                            st.inc(mm, "ct") if i_kc == n_kc - 1 else None
                V[f"C_y_{hp}_{qb}_0"] = st.cnt("ct") - 1
                V[f"C_y_{hp}_{qb}_1"] = st.cnt("ct")

                @block.vector
                def _(vector, hp=hp, qb=qb):
                    st.wait(vector, "ct", V[f"C_y_{hp}_{qb}_1"])
                    g = st.bankg.get(100)
                    if g is not None:
                        st.wait(vector, g[0], g[1])
                    vector.drain()
                    vector.tensor_copy(zstr[0:1, 0:512], BANK(4)[64:65, :])
                    vector.tensor_copy(zstr[0:1, 512:1024], BANK(5)[64:65, :])
                    vector.tensor_copy(ytmp_att[0:64, 0:512], BANK(4)[0:64, :])
                    cp = vector.tensor_copy(ytmp_att[64:128, 0:512], BANK(5)[0:64, :])
                    st.inc(cp, "cv")
                    st.bset(4, "cv", st.cnt("cv"))
                    st.bset(5, "cv", st.cnt("cv"))
                    vector.drain()
                    st.inc(vector.reciprocal(zstr[0:1, 0:1024], zstr[0:1, 0:1024]),
                           "cv")
                V[f"C_z_{hp}_{qb}"] = st.cnt("cv")

                def make_zb(hp=hp, qb=qb):
                    def emit():
                        @block.tensor
                        def _(tensor):
                            st.wait(tensor, "cv", V[f"C_z_{hp}_{qb}"])
                            st.bwait(tensor, 6)
                            tensor.matmul(
                                BANK(6)[0:64, :], ind2_sb[0:1, 0:64],
                                zstr[0:1, 0:512],
                                start=True, stop=True, tile_position=(0, 0))
                            st.inc(tensor.matmul(
                                BANK(6)[64:128, :], ind2_sb[0:1, 0:64],
                                zstr[0:1, 512:1024],
                                start=True, stop=True, tile_position=(0, 64)), "ct")
                        V[f"C_zb_{hp}_{qb}"] = st.cnt("ct")
                        st.bset(100, "ct", st.cnt("ct"))

                        @block.vector
                        def _(vector):
                            st.wait(vector, "ct", V[f"C_zb_{hp}_{qb}"])
                            st.inc(vector.tensor_tensor(
                                yall[:, hp * S + qb * 512: hp * S + (qb + 1) * 512],
                                ytmp_att[:, 0:512],
                                BANK(6), ALU.mult), "cv")
                            st.bset(6, "cv", st.cnt("cv"))
                    return emit
                pending_zb.append(make_zb())
                V["C_prev_attv"] = st.cnt("ct")
        while pending_zb:
            pending_zb.pop(0)()
        V["C_DONE"] = st.cnt("cv")

        if upto == "C":
            _finish(nc, block, st, ap_dbg, dump, locals(), ap_out, xring)
            s_c2.close(); s_cd.close(); s_bc.close()
            return nc

        s_c2.close()
        s_d = ExitStack()
        db_ = lambda n, sh, dt=F32: s_d.enter_context(nc.sbuf_tensor(n, sh, dt))
        x1t = [db_(f"d_x1_{i}", [128, D]) for i in range(3)]
        gring8 = db_("d_g8", [128, D])
        gev = db_("d_gev", [128, E])
        d_m1 = db_("d_m1", [128, 1])
        d_negm1 = db_("d_negm1", [128, 1])
        d_ismax = db_("d_ismax", [128, E])
        d_l2 = db_("d_l2", [128, E])
        d_m2 = db_("d_m2", [128, 1])
        d_expw = db_("d_expw", [128, E])
        d_maskw = db_("d_maskw", [128, E])
        d_wm = db_("d_wm", [128, E])
        d_zs = db_("d_zs", [128, 1])
        d_zr = db_("d_zr", [128, 1])
        ln3o = [lring[1], db_("d_ln3b", [128, D])]

        def d_head(tt):
            sem = f"ld{tt % 2}"

            @block.sync
            def _(sync, tt=tt, sem=sem):
                if tt == 0:
                    st.wait(sync, "cv", V["C_DONE"])
                if tt >= 2:
                    st.wait(sync, "cv", V[f"D_x1_{tt - 2}"])
                st.inc(sync.dma_start(out=xring[tt % 2][:],
                                      in_=ap_x[tt * 128:(tt + 1) * 128, :]), sem, 16)
            V[f"D_ld_{tt}"] = st.cnt(sem)

            ob = 0 if tt % 2 == 0 else 2
            @block.tensor
            def _(tensor, tt=tt, ob=ob):
                if tt == 0:
                    st.wait(tensor, "cv", V["C_DONE"])
                st.bwait(tensor, ob)
                st.bwait(tensor, ob + 1)
                for db in range(2):
                    for kc in range(NKC):
                        mm = tensor.matmul(
                            BANK(ob + db),
                            yall[:, kc * S + tt * 128: kc * S + (tt + 1) * 128],
                            wo_sb[:, kc * D + db * 512: kc * D + db * 512 + 512],
                            start=(kc == 0), stop=(kc == NKC - 1))
                st.inc(mm, "ct")
            V[f"D_omm_{tt}"] = st.cnt("ct")

            def preo(vector, tt=tt):
                st.wait(vector, "ct", V[f"D_omm_{tt}"])
            ln_stats(tt % 2, psum[:, ob * 512: ob * 512 + 1024], f"D2_{tt}", preo)
            ln_rstd(tt % 2, f"D2_{tt}")

        def d_tail(u):
            pu = u % 2
            ob = 0 if u % 2 == 0 else 2
            attn_ap = psum[:, ob * 512: ob * 512 + 1024]
            L = lnS[pu]

            @block.vector
            def _(vector, u=u, pu=pu, attn_ap=attn_ap, L=L):
                st.wait(vector, "cs", V[f"D2_{u}_rstd"])
                st.wait(vector, f"ld{u % 2}", V[f"D_ld_{u}"])
                if u >= 3:
                    st.wait(vector, f"sp{u % 3}", V[f"D_sp_{u - 3}"])
                vector.tensor_scalar(out=lring[0][:, :], in0=attn_ap,
                                     scalar1=L.tmp2[:, 0:1], scalar2=L.sr[:, :],
                                     op0=ALU.subtract, op1=ALU.mult)
                vector.drain()
                st.inc(vector.tensor_tensor(x1t[u % 3][:, :], lring[0][:, :],
                                            xring[u % 2][:, :], ALU.add), "cv")
            V[f"D_x1_{u}"] = st.cnt("cv")
            st.bset(ob, "cv", st.cnt("cv"))
            st.bset(ob + 1, "cv", st.cnt("cv"))

            @block.scalar
            def _(scalar, u=u):
                st.wait(scalar, "cv", V[f"D_x1_{u}"])
                st.inc(scalar.dma_start(out=ap_x1s[u * 128:(u + 1) * 128, :],
                                        in_=x1t[u % 3][:, :]), f"sp{u % 3}", 16)
            V[f"D_sp_{u}"] = st.cnt(f"sp{u % 3}")

            ln_stats(pu, x1t[u % 3][:, :], f"D3_{u}", None)
            ln_rstd(pu, f"D3_{u}")

            def pre3(vector, u=u):
                if u >= 2:
                    st.wait(vector, "ct", V[f"D_tpall_{u - 2}"])
            ln_norm(pu, x1t[u % 3][:, :], ln3o[pu][:, :], f"D3_{u}", pre3)

            for kc in range(NKC):
                @block.tensor
                def _(tensor, u=u, kc=kc, pu=pu):
                    if kc == 0:
                        st.wait(tensor, "cv", V[f"D3_{u}_ln"])
                    b = 4 + kc % 2
                    st.bwait(tensor, b)
                    st.inc(tensor.transpose(BANK(b, 128),
                                            ln3o[pu][:, kc * 128:(kc + 1) * 128],
                                            eye[:, :]), "ct")
                    V[f"D_tp_{u}_{kc}"] = st.cnt("ct")

                @block.vector
                def _(vector, u=u, kc=kc):
                    if kc == 0 and u >= 1:
                        st.wait(vector, "ct", V[f"D_gmm_{u - 1}"])
                    st.wait(vector, "ct", V[f"D_tp_{u}_{kc}"])
                    vector.tensor_copy(
                        xfm[:, kc * S + u * 128: kc * S + (u + 1) * 128],
                        BANK(4 + kc % 2, 128))
                    cp = vector.tensor_copy(gring8[:, kc * 128:(kc + 1) * 128],
                                            BANK(4 + kc % 2, 128))
                    st.inc(cp, "cv")
                    st.bset(4 + kc % 2, "cv", st.cnt("cv"))
                V[f"D_tpev_{u}_{kc}"] = st.cnt("cv")
            V[f"D_tpall_{u}"] = st.cnt("ct")
            V[f"D_tpev_{u}"] = st.cnt("cv")

            @block.tensor
            def _(tensor, u=u):
                st.wait(tensor, "cv", V[f"D_tpev_{u}"])
                st.bwait(tensor, 6)
                for kc in range(NKC):
                    mm = tensor.matmul(BANK(6, 8), gring8[:, kc * 128:(kc + 1) * 128],
                                       gwT_sb[:, kc * E:(kc + 1) * E],
                                       start=(kc == 0), stop=(kc == NKC - 1))
                st.inc(mm, "ct")
            V[f"D_gmm_{u}"] = st.cnt("ct")

            @block.vector
            def _(vector, u=u):
                st.wait(vector, "ct", V[f"D_gmm_{u}"])
                vector.drain()
                vector.tensor_copy(gev[:, :], BANK(6, 8))
                vector.drain()
                vector.reduce_max(d_m1[:, :], gev[:, :], axis=AX)
                vector.drain()
                vector.tensor_scalar(out=d_ismax[:, :], in0=gev[:, :],
                                     scalar1=d_m1[:, :], scalar2=None, op0=ALU.is_ge)
                vector.tensor_scalar(out=d_negm1[:, :], in0=d_m1[:, :],
                                     scalar1=-1.0, scalar2=None, op0=ALU.mult)
                vector.drain()
                vector.scalar_tensor_tensor(out=d_l2[:, :], in0=d_ismax[:, :],
                                            scalar=-1e30, in1=gev[:, :],
                                            op0=ALU.mult, op1=ALU.add)
                vector.drain()
                vector.reduce_max(d_m2[:, :], d_l2[:, :], axis=AX)
                vector.drain()
                st.inc(vector.tensor_scalar(out=d_maskw[:, :], in0=gev[:, :],
                                            scalar1=d_m2[:, :], scalar2=None,
                                            op0=ALU.is_ge), "cv")
                st.bset(6, "cv", st.cnt("cv"))
            V[f"D_cm1_{u}"] = st.cnt("cv")

            @block.scalar
            def _(scalar, u=u):
                st.wait(scalar, "cv", V[f"D_cm1_{u}"])
                st.inc(scalar.activation(out=d_expw[:, :], in_=gev[:, :], func=AF.Exp,
                                         bias=d_negm1[:, :], scale=1.0), "cs")
            V[f"D_cexp_{u}"] = st.cnt("cs")

            @block.vector
            def _(vector, u=u):
                st.wait(vector, "cs", V[f"D_cexp_{u}"])
                vector.tensor_tensor(d_wm[:, :], d_expw[:, :], d_maskw[:, :], ALU.mult)
                vector.drain()
                vector.reduce_sum(d_zs[:, :], d_wm[:, :], axis=AX)
                vector.drain()
                vector.reciprocal(d_zr[:, :], d_zs[:, :])
                vector.drain()
                st.inc(vector.tensor_scalar(
                    out=comb_sb[:, u * (E + 1): u * (E + 1) + E],
                    in0=d_wm[:, :], scalar1=d_zr[:, :], scalar2=None,
                    op0=ALU.mult), "cv")
            V[f"D_comb_{u}"] = st.cnt("cv")
            V[f"D_gev_{u}"] = st.cnt("cv")

        for tt in range(NTT + 1):
            if tt < NTT:
                d_head(tt)
            if tt >= 1:
                d_tail(tt - 1)

        for tt in range(NTT):
            @block.tensor
            def _(tensor, tt=tt):
                if tt == 0:
                    st.wait(tensor, "cv", V[f"D_comb_{NTT - 1}"])
                    st.wait(tensor, "cg", V["CG_INIT"])
                st.bwait(tensor, 7)
                st.inc(tensor.transpose(BANK(7, 128)[0:E + 1, :],
                                        comb_sb[:, tt * (E + 1):(tt + 1) * (E + 1)],
                                        eye[:, :]), "ct")
            V[f"D_cT_{tt}"] = st.cnt("ct")

            @block.vector
            def _(vector, tt=tt):
                st.wait(vector, "ct", V[f"D_cT_{tt}"])
                st.inc(vector.tensor_copy(cT_sb[:, tt * 128:(tt + 1) * 128],
                                          BANK(7, 128)[0:E + 1, :]), "cv")
                st.bset(7, "cv", st.cnt("cv"))
        V["D_DONE"] = st.cnt("cv")

        if upto == "D":
            _finish(nc, block, st, ap_dbg, dump, locals(), ap_out, xring)
            s_d.close(); s_c2.close(); s_cd.close(); s_bc.close()
            return nc

        s_d.close()
        s_cd.close()
        s_bc.close()
        w1r = [sb(f"f_w1_{i}", [128, NKC * 2048], BF16) for i in range(2)]
        w2r = [sb(f"f_w2_{i}", [128, 16 * D], BF16) for i in range(2)]
        h_sb = sb("f_h", [128, 16 * 256], BF16)
        ytr = [sb(f"f_yt_{i}", [128, 512]) for i in range(2)]
        x1h = [sb(f"g_x1_{i}", [128, 512]) for i in range(2)]

        def emit_wload(pi):
            ex, hf = pi // 2, pi % 2
            slot = pi % 2
            wsem = f"ws{slot}"

            @block.gpsimd
            def _(gpsimd):
                if pi >= 2:
                    st.wait(gpsimd, "ct", V[f"E_done_{pi - 2}"])
                else:
                    st.wait(gpsimd, "cv", V["D_DONE"])
                    for sx in range(3):
                        st.wait(gpsimd, f"sp{sx}", st.cnt(f"sp{sx}"))
                st.inc(gpsimd.dma_start(
                    out=w1r[slot][:].rearrange("p (kc n) -> p kc n", n=2048),
                    in_=ap_w1[ex].rearrange("(kc p) n -> p kc n", p=128)[:, :, hf * 2048:(hf + 1) * 2048]),
                    wsem, 16)
                st.inc(gpsimd.dma_start(
                    out=w2r[slot][:].rearrange("p (fc n) -> p fc n", n=D),
                    in_=ap_w2[ex].rearrange("(fc p) n -> p fc n", p=128)[:, hf * 16:(hf + 1) * 16, :]),
                    wsem, 16)
            V[f"E_wld_{pi}"] = st.cnt(wsem)

        emit_wload(0)
        emit_wload(1)

        for tt in range(NTT):
            for db in range(2):
                yslot = (2 * tt + db) % 2
                b = 4 + (2 * tt + db) % 3
                @block.tensor
                def _(tensor, tt=tt, db=db, b=b):
                    if tt == 0 and db == 0:
                        st.wait(tensor, "cv", V["D_DONE"])
                    st.bwait(tensor, b)
                    st.inc(tensor.matmul(BANK(b), cT_sb[:, tt * 128:(tt + 1) * 128],
                                         b2aug_sb[:, db * 512:(db + 1) * 512],
                                         start=True, stop=True), "ct")
                V[f"E_b2_{tt}_{db}"] = st.cnt("ct")

                @block.vector
                def _(vector, tt=tt, db=db, b=b, yslot=yslot):
                    st.wait(vector, "ct", V[f"E_b2_{tt}_{db}"])
                    g = st.bankg.get(200 + yslot)
                    if g is not None:
                        st.wait(vector, g[0], g[1])
                    st.inc(vector.tensor_copy(ytr[yslot][:, :], BANK(b)), "cv")
                    st.bset(b, "cv", st.cnt("cv"))
                V[f"E_b2ev_{tt}_{db}"] = st.cnt("cv")

                @block.gpsimd
                def _(gpsimd, tt=tt, db=db, yslot=yslot):
                    st.wait(gpsimd, "cv", V[f"E_b2ev_{tt}_{db}"])
                    st.inc(gpsimd.dma_start(
                        out=ap_facc[tt * 128:(tt + 1) * 128, db * 512:(db + 1) * 512],
                        in_=ytr[yslot][:, :]), f"fa{yslot}", 16)
                st.bset(200 + yslot, f"fa{yslot}", st.cnt(f"fa{yslot}"))

        yslot_i = 0
        NTB2 = 8
        hpair_guard = {}

        for pi in range(18):
            ex, hf = pi // 2, pi % 2
            slot = pi % 2
            wsem = f"ws{slot}"
            cs0 = st.cnt("cs")
            st.c["cs"] = cs0 + 8 * NTB2
            GEL = lambda tb, fp: cs0 + tb * 8 + fp + 1
            for tb in range(NTB2):
                for fp in range(8):
                    ps_ = fp % 2
                    @block.tensor
                    def _(tensor, pi=pi, tb=tb, fp=fp, ps_=ps_, slot=slot,
                          wsem=wsem, cs0=cs0):
                        if tb == 0 and fp == 0:
                            st.wait(tensor, wsem, V[f"E_wld_{pi}"])
                        gi = tb * 8 + fp - 2
                        if gi >= 0:
                            tensor.wait_ge(st.sems["cs"], cs0 + gi + 1)
                        elif hpair_guard.get(ps_) is not None:
                            tensor.wait_ge(st.sems["cs"], hpair_guard[ps_])
                        for fc in (2 * fp, 2 * fp + 1):
                            b = 2 * ps_ + (fc % 2)
                            for kc in range(NKC):
                                mm = tensor.matmul(
                                    BANK(b, 256),
                                    w1r[slot][:, kc * 2048 + fc * 128: kc * 2048 + (fc + 1) * 128],
                                    xfm[:, kc * S + tb * 256: kc * S + (tb + 1) * 256],
                                    start=(kc == 0), stop=(kc == NKC - 1))
                        st.inc(mm, "ct")
                        V[f"E_hmm_{pi}_{tb}_{fp}"] = st.cnt("ct")

                for tt2 in range(2):
                    tt_g = tb * 2 + tt2
                    for db in range(2):
                        b = 4 + (tt2 * 2 + db) % 3
                        @block.tensor
                        def _(tensor, pi=pi, tb=tb, tt2=tt2, db=db, b=b,
                              slot=slot, cs0=cs0):
                            if tt2 == 0 and db == 0:
                                st.wait(tensor, "cs", GEL(tb, 7))
                            st.bwait(tensor, b)
                            for fc in range(16):
                                mm = tensor.matmul(
                                    BANK(b),
                                    h_sb[:, fc * 256 + tt2 * 128: fc * 256 + tt2 * 128 + 128],
                                    w2r[slot][:, fc * D + db * 512: fc * D + db * 512 + 512],
                                    start=(fc == 0), stop=(fc == 15))
                            st.inc(mm, "ct")
                        V[f"E_ymm_{pi}_{tb}_{tt2}_{db}"] = st.cnt("ct")

                        yslot = yslot_i % 2
                        yslot_i += 1
                        @block.vector
                        def _(vector, pi=pi, ex=ex, tb=tb, tt2=tt2, db=db, b=b,
                              yslot=yslot, tt_g=tt_g):
                            st.wait(vector, "ct", V[f"E_ymm_{pi}_{tb}_{tt2}_{db}"])
                            g = st.bankg.get(200 + yslot)
                            if g is not None:
                                st.wait(vector, g[0], g[1])
                            if ex < E:
                                op = vector.tensor_scalar(
                                    out=ytr[yslot][:, :], in0=BANK(b),
                                    scalar1=comb_sb[:, tt_g * (E + 1) + ex: tt_g * (E + 1) + ex + 1],
                                    scalar2=None, op0=ALU.mult)
                            else:
                                op = vector.tensor_copy(ytr[yslot][:, :], BANK(b))
                            st.inc(op, "cv")
                            st.bset(b, "cv", st.cnt("cv"))
                        V[f"E_yev_{pi}_{tb}_{tt2}_{db}"] = st.cnt("cv")

                        @block.gpsimd
                        def _(gpsimd, pi=pi, tb=tb, tt2=tt2, db=db, yslot=yslot,
                              tt_g=tt_g):
                            st.wait(gpsimd, "cv", V[f"E_yev_{pi}_{tb}_{tt2}_{db}"])
                            st.inc(gpsimd.dma_start(
                                out=ap_facc[tt_g * 128:(tt_g + 1) * 128,
                                            db * 512:(db + 1) * 512],
                                in_=ytr[yslot][:, :], accum_op=ALU.add),
                                f"fa{yslot}", 16)
                        st.bset(200 + yslot, f"fa{yslot}", st.cnt(f"fa{yslot}"))

            @block.scalar
            def _(scalar, pi=pi, ex=ex, hf=hf, cs0=cs0):
                for tb in range(NTB2):
                    for fp in range(8):
                        ps_ = fp % 2
                        scalar.wait_ge(st.sems["ct"], V[f"E_hmm_{pi}_{tb}_{fp}"])
                        scalar.activation(
                            out=h_sb[:, fp * 512:(fp + 1) * 512].rearrange(
                                "p (b n) -> p b n", n=256),
                            in_=psum[:, ps_ * 1024: ps_ * 1024 + 1024].rearrange(
                                "p (b n) -> p b n", n=512)[:, :, 0:256],
                            func=AF.Gelu,
                            bias=b1_sb[:, ex * 32 + hf * 16 + 2 * fp: ex * 32 + hf * 16 + 2 * fp + 1],
                            scale=1.0).then_inc(st.sems["cs"], 1)
            hpair_guard[0] = cs0 + 8 * NTB2 - 1
            hpair_guard[1] = cs0 + 8 * NTB2

            V[f"E_done_{pi}"] = st.cnt("ct")
            if pi + 2 < 18:
                emit_wload(pi + 2)

        if upto == "E":
            _finish(nc, block, st, ap_dbg, dump, locals(), ap_out, xring)
            return nc

        fring = [xring[0], xring[1]]
        o_t = [lring[0], lring[1]]
        x1p = [[x1h[0], x1h[1]], [ytr[0], ytr[1]]]
        FA_TOT = {x: st.cnt(f"fa{x}") for x in range(2)}
        SP_TOT = {x: st.cnt(f"sp{x}") for x in range(3)}

        def f_head(tt):
            fs = f"lf{tt % 2}"
            xs = f"lx{tt % 2}"

            @block.sync
            def _(sync, tt=tt, fs=fs, xs=xs):
                if tt == 0:
                    for x in range(2):
                        st.wait(sync, f"fa{x}", FA_TOT[x])
                    for x in range(3):
                        st.wait(sync, f"sp{x}", SP_TOT[x])
                if tt >= 2:
                    st.wait(sync, "cv", V[f"F_o_{tt - 2}"])
                st.inc(sync.dma_start(out=fring[tt % 2][:],
                                      in_=ap_facc[tt * 128:(tt + 1) * 128, :]), fs, 16)
                st.inc(sync.dma_start(out=x1p[tt % 2][0][:, :],
                                      in_=ap_x1s[tt * 128:(tt + 1) * 128, 0:512]), xs, 16)
                st.inc(sync.dma_start(out=x1p[tt % 2][1][:, :],
                                      in_=ap_x1s[tt * 128:(tt + 1) * 128, 512:1024]), xs, 16)
            V[f"F_lf_{tt}"] = st.cnt(fs)
            V[f"F_lx_{tt}"] = st.cnt(xs)

            def pref(vector, tt=tt, fs=fs):
                st.wait(vector, fs, V[f"F_lf_{tt}"])
            ln_stats(tt % 2, fring[tt % 2][:, :], f"F_{tt}", pref)
            ln_rstd(tt % 2, f"F_{tt}")

        def f_tail(u):
            pu = u % 2
            L = lnS[pu]

            @block.vector
            def _(vector, u=u, pu=pu, L=L):
                st.wait(vector, "cs", V[f"F_{u}_rstd"])
                st.wait(vector, f"lx{pu}", V[f"F_lx_{u}"])
                if u >= 2:
                    st.wait(vector, f"so{u % 2}", V[f"F_st_{u - 2}"])
                vector.tensor_scalar(out=o_t[pu][:, :], in0=fring[pu][:, :],
                                     scalar1=L.tmp2[:, 0:1], scalar2=L.sr[:, :],
                                     op0=ALU.subtract, op1=ALU.mult)
                vector.drain()
                vector.tensor_tensor(o_t[pu][:, 0:512], o_t[pu][:, 0:512],
                                     x1p[pu][0][:, :], ALU.add)
                st.inc(vector.tensor_tensor(o_t[pu][:, 512:1024],
                                            o_t[pu][:, 512:1024],
                                            x1p[pu][1][:, :], ALU.add), "cv")
            V[f"F_o_{u}"] = st.cnt("cv")

            @block.sync
            def _(sync, u=u):
                st.wait(sync, "cv", V[f"F_o_{u}"])
                st.inc(sync.dma_start(out=ap_out[u * 128:(u + 1) * 128, :],
                                      in_=o_t[u % 2][:, :]), f"so{u % 2}", 16)
            V[f"F_st_{u}"] = st.cnt(f"so{u % 2}")

        for tt in range(NTT + 1):
            if tt < NTT:
                f_head(tt)
            if tt >= 1:
                f_tail(tt - 1)

        @block.sync
        def _(sync):
            sync.wait_ge(st.sems["so0"], st.cnt("so0"))
            sync.wait_ge(st.sems["so1"], st.cnt("so1"))

    return nc


def _finish(nc, block, st, ap_dbg, dump, scope, ap_out, xring):
    @block.sync
    def _(sync):
        for k in ["cv", "cs", "ct"]:
            if st.cnt(k):
                sync.wait_ge(st.sems[k], st.cnt(k))
        n = 0
        if ap_dbg is not None:
            name = dump[0]
            buf = scope[name]
            rows = dump[1][0]
            st.inc(sync.dma_start(out=ap_dbg[:, :], in_=buf[0:rows, 0:dump[1][1]]),
                   "st_o", 16)
            n += 1
        for tt in range(NTT):
            st.inc(sync.dma_start(out=ap_out[tt * 128:(tt + 1) * 128, :],
                                  in_=xring[tt % 2][:, :]), "st_o", 16)
        sync.wait_ge(st.sems["st_o"], st.cnt("st_o"))


def make_inputs_per_core(inputs):
    cos2, ssin2 = _host_tables()
    masks = _host_masks()
    eye = np.eye(128, dtype=np.float32)
    cvals = np.zeros((128, 8), dtype=np.float32)
    cvals[:, 0] = EPS
    cvals[:, 1] = 0.125
    cvals[:, 2] = -10.0
    cvals[:, 3] = 1.0
    cvals[:, 4] = 0.0
    cvals[:, 5] = -0.5
    ind2 = np.zeros((2, 128), dtype=np.float32)
    ind2[0, 0:64] = 1.0
    ind2[1, 64:128] = 1.0

    w1 = np.concatenate([np.asarray(inputs["L0_ffn_W1"]),
                         np.asarray(inputs["L0_ffn_sW1"])[None]], axis=0)
    w2 = np.concatenate([np.asarray(inputs["L0_ffn_W2"]),
                         np.asarray(inputs["L0_ffn_sW2"])[None]], axis=0)
    b1 = np.concatenate([np.asarray(inputs["L0_ffn_b1"]),
                         np.asarray(inputs["L0_ffn_sb1"])[None]], axis=0)
    b1c = np.ascontiguousarray(b1.reshape(E + 1, 32, 128).transpose(2, 0, 1)
                               .reshape(128, (E + 1) * 32))
    b2aug = np.ascontiguousarray(
        np.concatenate([np.asarray(inputs["L0_ffn_b2"]),
                        np.asarray(inputs["L0_ffn_sb2"])[None]], axis=0))
    gwT = np.ascontiguousarray(np.asarray(inputs["L0_ffn_gate_w"]).T)

    shared = {
        "Wq": np.ascontiguousarray(np.asarray(inputs["L0_attn_Wq"])),
        "Wk": np.ascontiguousarray(np.asarray(inputs["L0_attn_Wk"])),
        "Wv": np.ascontiguousarray(np.asarray(inputs["L0_attn_Wv"])),
        "Wo": np.ascontiguousarray(np.asarray(inputs["L0_attn_Wo"])),
        "gwT": gwT, "W1": np.ascontiguousarray(w1), "W2": np.ascontiguousarray(w2),
        "b1c": b1c, "b2aug": b2aug, "costab": cos2, "ssintab": ssin2,
        "masks": masks, "eye": eye, "consts": cvals, "ind2": ind2,
    }
    x = np.asarray(inputs["L0_attn_x"])
    in_maps = []
    for c in range(8):
        m = dict(shared)
        m["x"] = np.ascontiguousarray(x[c])
        in_maps.append(m)
    return in_maps


def kernel(**inputs):
    in_maps = make_inputs_per_core(inputs)
    nc = build_program("full")
    res = run_bass_kernel_spmd(nc, in_maps, core_ids=list(range(8)))
    out = np.stack([np.asarray(res.results[c]["out"]) for c in range(8)], axis=0)
    return out.astype(np.float32)


# version 11
# speedup vs baseline: 1.0330x; 1.0139x over previous
import numpy as np
import ml_dtypes
from contextlib import ExitStack

import concourse.bass as bass
import concourse.mybir as mybir
from concourse.bass_utils import run_bass_kernel_spmd

F32 = mybir.dt.float32
BF16 = mybir.dt.bfloat16
AF = mybir.ActivationFunctionType
ALU = mybir.AluOpType
AX = mybir.AxisListType.X

S, D = 2048, 1024
NH, KVH, HD = 16, 4, 64
E, FF = 8, 4096
NTT, NTB, NKC = 16, 4, 8
EPS = 1e-5


def _host_tables():
    half = HD // 2
    try:
        import jax
        cpu = jax.devices("cpu")[0]
        import jax.numpy as jnp
        with jax.default_device(cpu):
            inv_freq = 1.0 / 10000.0 ** (jnp.arange(half, dtype=jnp.float32) / half)
            t = jnp.arange(S, dtype=jnp.float32)
            freqs = jnp.einsum("i,j->ij", t, inv_freq)
            emb = jnp.concatenate((freqs, freqs), axis=-1)
            cos = np.asarray(jnp.cos(emb), dtype=np.float32)
            sin = np.asarray(jnp.sin(emb), dtype=np.float32)
    except Exception:
        inv_freq = (1.0 / (np.float32(10000.0) **
                           (np.arange(half, dtype=np.float32) / np.float32(half)))).astype(np.float32)
        t = np.arange(S, dtype=np.float32)
        freqs = (t[:, None] * inv_freq[None, :]).astype(np.float32)
        emb = np.concatenate((freqs, freqs), axis=-1)
        cos = np.cos(emb).astype(np.float32)
        sin = np.sin(emb).astype(np.float32)
    cos_fm = cos.T.copy()
    sin_fm = sin.T.copy()
    ssin = sin_fm.copy()
    ssin[:32] = -sin_fm[:32]
    return (np.ascontiguousarray(np.concatenate([cos_fm, cos_fm], axis=0)),
            np.ascontiguousarray(np.concatenate([ssin, ssin], axis=0)))


def _host_masks():
    p = np.arange(128)[:, None]
    j = np.arange(512)[None, :]
    ms = [(128 * r + p <= j).astype(np.float32) for r in range(4)]
    return np.ascontiguousarray(np.concatenate(ms, axis=1).astype(ml_dtypes.bfloat16))


class St:
    def __init__(self):
        self.c = {}
        self.sems = {}
        self.bankg = {}

    def inc(self, instr, key, n=1):
        instr.then_inc(self.sems[key], n)
        self.c[key] = self.c.get(key, 0) + n
        return self.c[key]

    def cnt(self, key):
        return self.c.get(key, 0)

    def wait(self, eng, key, val):
        if val and val > 0:
            eng.wait_ge(self.sems[key], val)

    def bwait(self, eng, b):
        g = self.bankg.get(b)
        if g is not None:
            self.wait(eng, g[0], g[1])

    def bset(self, b, key, val):
        self.bankg[b] = (key, val)


def build_program(upto="full", dump=None):
    nc = bass.Bass("TRN2", target_bir_lowering=False)
    ap_x = nc.dram_tensor("x", [S, D], F32, kind="ExternalInput").ap()
    ap_wq = nc.dram_tensor("Wq", [D, D], F32, kind="ExternalInput").ap()
    ap_wk = nc.dram_tensor("Wk", [D, 256], F32, kind="ExternalInput").ap()
    ap_wv = nc.dram_tensor("Wv", [D, 256], F32, kind="ExternalInput").ap()
    ap_wo = nc.dram_tensor("Wo", [D, D], F32, kind="ExternalInput").ap()
    ap_gwT = nc.dram_tensor("gwT", [D, E], F32, kind="ExternalInput").ap()
    ap_w1 = nc.dram_tensor("W1", [E + 1, D, FF], F32, kind="ExternalInput").ap()
    ap_w2 = nc.dram_tensor("W2", [E + 1, FF, D], F32, kind="ExternalInput").ap()
    ap_b1 = nc.dram_tensor("b1c", [128, (E + 1) * 32], F32, kind="ExternalInput").ap()
    ap_b2aug = nc.dram_tensor("b2aug", [E + 1, D], F32, kind="ExternalInput").ap()
    ap_cos = nc.dram_tensor("costab", [128, S], F32, kind="ExternalInput").ap()
    ap_ssin = nc.dram_tensor("ssintab", [128, S], F32, kind="ExternalInput").ap()
    ap_masks = nc.dram_tensor("masks", [128, 2048], BF16, kind="ExternalInput").ap()
    ap_eye = nc.dram_tensor("eye", [128, 128], F32, kind="ExternalInput").ap()
    ap_consts = nc.dram_tensor("consts", [128, 8], F32, kind="ExternalInput").ap()
    ap_ind2 = nc.dram_tensor("ind2", [2, 128], F32, kind="ExternalInput").ap()
    ap_out = nc.dram_tensor("out", [S, D], F32, kind="ExternalOutput").ap()
    ap_x1s = nc.dram_tensor("x1scr", [S, D], F32, kind="Internal").ap()
    ap_facc = nc.dram_tensor("ffnacc", [S, D], F32, kind="Internal").ap()
    ap_dbg = None
    if dump is not None:
        ap_dbg = nc.dram_tensor("dbg", list(dump[1]), dump[2], kind="ExternalOutput").ap()

    st = St()
    V = st.c

    with ExitStack() as stack:
        en = stack.enter_context
        sb = lambda n, sh, dt=F32: en(nc.sbuf_tensor(n, sh, dt))

        consts = sb("c_consts", [128, 8])
        eye = sb("c_eye", [128, 128])
        b1_sb = sb("c_b1", [128, (E + 1) * 32])
        b2aug_sb = sb("c_b2aug", [E + 1, D])
        gwT_sb = sb("c_gwT", [128, NKC * E])
        ind2_sb = sb("c_ind2", [2, 128])
        xfm = sb("p_xfm", [128, NKC * S], BF16)
        comb_sb = sb("p_comb", [128, NTT * (E + 1)])
        cT_sb = sb("p_cT", [E + 1, S])
        class _LnScr:
            def __init__(self, i):
                self.st6 = sb(f"ln{i}_st6", [128, 12])
                self.st2a = sb(f"ln{i}_st2a", [128, 2])
                self.st2b = sb(f"ln{i}_st2b", [128, 2])
                self.tmp2 = sb(f"ln{i}_tmp2", [128, 2])
                self.d1 = sb(f"ln{i}_d1", [128, 1])
                self.var1 = sb(f"ln{i}_var1", [128, 1])
                self.sstd = sb(f"ln{i}_sstd", [128, 1])
                self.sr = sb(f"ln{i}_sr", [128, 1])
        lnS = [_LnScr(0), _LnScr(1)]
        xring = [sb(f"r_x{i}", [128, D]) for i in range(2)]
        lring = [sb(f"r_ln{i}", [128, D]) for i in range(2)]

        s_bc = ExitStack()
        bcb = lambda n, sh, dt=F32: s_bc.enter_context(nc.sbuf_tensor(n, sh, dt))
        wo_sb = bcb("a_wo", [128, NKC * D], BF16)
        q_sb = bcb("a_q", [128, 8 * S], BF16)
        k2_sb = bcb("a_k2", [128, KVH * S], BF16)
        v_sb = bcb("a_v", [128, NTT * KVH * 65], BF16)

        s_b = ExitStack()
        bb = lambda n, sh, dt=F32: s_b.enter_context(nc.sbuf_tensor(n, sh, dt))
        costab = bb("a_cos", [128, S])
        ssintab = bb("a_ssin", [128, S])
        wq_sb = bb("a_wq", [128, NKC * D], BF16)
        wk_sb = bb("a_wk", [128, NKC * 256], BF16)
        wv_sb = bb("a_wv", [128, NKC * 256], BF16)
        qraw = [bb(f"r_qraw{i}", [128, S]) for i in range(2)]
        qswp = [bb(f"r_qswp{i}", [128, S]) for i in range(2)]

        psum = en(nc.psum_tensor("psum", [128, 4096], F32))
        BANK = lambda b, w=512: psum[:, b * 512: b * 512 + w]

        for name in ["ld0", "ld1", "ld2", "ld3", "st_o", "ws0", "ws1", "cst",
                     "swp0", "swp1", "sp0", "sp1", "sp2", "fa0", "fa1", "fa2", "fa3",
                     "lf0", "lf1", "lx0", "lx1", "so0", "so1", "cv", "cs", "ct", "cg"]:
            st.sems[name] = en(nc.semaphore(name))

        nc.const_aps.register(F32, EPS, consts[:, 0:1])
        nc.const_aps.register(F32, 0.125, consts[:, 1:2])
        nc.const_aps.register(F32, -10.0, consts[:, 2:3])
        nc.const_aps.register(F32, 1.0, consts[:, 3:4])
        nc.const_aps.register(F32, 0.0, consts[:, 4:5])
        nc.const_aps.register(F32, -0.5, consts[:, 5:6])

        block = en(nc.Block())

        @block.scalar
        def _(scalar):
            st.inc(scalar.dma_start(out=consts[:], in_=ap_consts[:, :]), "cst", 16)
            st.inc(scalar.dma_start(out=ind2_sb[:], in_=ap_ind2[:, :]), "cst", 16)
            st.inc(scalar.dma_start(out=eye[:], in_=ap_eye[:, :]), "cst", 16)
            st.inc(scalar.dma_start(out=costab[:], in_=ap_cos[:, :]), "cst", 16)
            st.inc(scalar.dma_start(out=ssintab[:], in_=ap_ssin[:, :]), "cst", 16)
            st.inc(scalar.dma_start(out=b1_sb[:], in_=ap_b1[:, :]), "cst", 16)
            st.inc(scalar.dma_start(out=b2aug_sb[:], in_=ap_b2aug[:, :]), "cst", 16)
            st.inc(scalar.dma_start(out=gwT_sb[:].rearrange("p (kc n) -> p kc n", n=E),
                                    in_=ap_gwT.rearrange("(kc p) n -> p kc n", p=128)),
                   "cst", 16)
        V["CST_ALL"] = st.cnt("cst")

        @block.gpsimd
        def _(gpsimd):

            gpsimd.memset(
                comb_sb[:].rearrange("p (tt e) -> p tt e", e=E + 1)[:, :, E:E + 1], 1.0)
            st.inc(gpsimd.memset(
                v_sb[:].rearrange("p (tt k d) -> p tt k d", k=KVH, d=65)[:, :, :, 64:65],
                1.0), "cg", 1)
            st.inc(gpsimd.dma_start(
                out=wq_sb[:].rearrange("p (kc n) -> p kc n", n=D),
                in_=ap_wq.rearrange("(kc p) n -> p kc n", p=128)), "ws0", 16)
            st.inc(gpsimd.dma_start(
                out=wk_sb[:].rearrange("p (kc n) -> p kc n", n=256),
                in_=ap_wk.rearrange("(kc p) n -> p kc n", p=128)), "ws0", 16)
            st.inc(gpsimd.dma_start(
                out=wv_sb[:].rearrange("p (kc n) -> p kc n", n=256),
                in_=ap_wv.rearrange("(kc p) n -> p kc n", p=128)), "ws0", 16)
            st.inc(gpsimd.dma_start(
                out=wo_sb[:].rearrange("p (kc n) -> p kc n", n=D),
                in_=ap_wo.rearrange("(kc p) n -> p kc n", p=128)), "ws0", 16)
        V["CG_INIT"] = st.cnt("cg")
        V["WS_QKVO"] = st.cnt("ws0")

        @block.scalar
        def _(scalar):
            st.wait(scalar, "cst", V["CST_ALL"])

        ln_guard = {}

        def ln_stats(p_, tile_ap, tag, pre=None):
            L = lnS[p_]

            @block.vector
            def _(vector):
                g = ln_guard.get(p_)
                if g is not None:
                    st.wait(vector, "cs", g)
                if pre is not None:
                    pre(vector)
                vector.drain()
                vector.bn_stats(L.st6[:, 0:6], tile_ap[:, 0:512])
                vector.bn_stats(L.st6[:, 6:12], tile_ap[:, 512:1024])
                vector.drain()
                vector.bn_aggr(L.st2a[:, :], L.st6[:, 0:6])
                vector.bn_aggr(L.st2b[:, :], L.st6[:, 6:12])
                vector.drain()
                vector.tensor_tensor(L.tmp2[:, :], L.st2a[:, :], L.st2b[:, :], ALU.add)
                vector.tensor_tensor(L.d1[:, :], L.st2a[:, 0:1], L.st2b[:, 0:1],
                                     ALU.subtract)
                vector.drain()
                vector.tensor_scalar(out=L.tmp2[:, :], in0=L.tmp2[:, :], scalar1=0.5,
                                     scalar2=None, op0=ALU.mult)
                vector.tensor_tensor(L.d1[:, :], L.d1[:, :], L.d1[:, :], ALU.mult)
                vector.drain()
                st.inc(vector.scalar_tensor_tensor(
                    out=L.var1[:, :], in0=L.d1[:, :], scalar=0.25,
                    in1=L.tmp2[:, 1:2], op0=ALU.mult, op1=ALU.add), "cv")
            V[f"{tag}_var"] = st.cnt("cv")

        def ln_rstd(p_, tag):
            L = lnS[p_]

            @block.scalar
            def _(scalar):
                st.wait(scalar, "cv", V[f"{tag}_var"])
                scalar.activation(out=L.sstd[:, :], in_=L.var1[:, :],
                                  func=AF.Ln, bias=EPS, scale=1.0)
                scalar.drain()
                st.inc(scalar.activation(out=L.sr[:, :], in_=L.sstd[:, :],
                                         func=AF.Exp, bias=0.0, scale=-0.5), "cs")
            V[f"{tag}_rstd"] = st.cnt("cs")
            ln_guard[p_] = st.cnt("cs")

        def ln_norm(p_, tile_ap, out_ap, tag, pre=None):
            L = lnS[p_]

            @block.vector
            def _(vector):
                st.wait(vector, "cs", V[f"{tag}_rstd"])
                if pre is not None:
                    pre(vector)
                st.inc(vector.tensor_scalar(
                    out=out_ap, in0=tile_ap, scalar1=L.tmp2[:, 0:1],
                    scalar2=L.sr[:, :], op0=ALU.subtract, op1=ALU.mult), "cv")
            V[f"{tag}_ln"] = st.cnt("cv")

        def a_tail(u):
            def pre(vector, u=u):
                if u >= 2:
                    st.wait(vector, "ct", V[f"A_tp_{u - 2}"])
            ln_norm(u % 2, xring[u % 2][:, :], lring[u % 2][:, :], f"A{u}", pre)
            for kc in range(NKC):
                @block.tensor
                def _(tensor, u=u, kc=kc):
                    if kc == 0:
                        st.wait(tensor, "cv", V[f"A{u}_ln"])
                    b = kc % 4
                    st.bwait(tensor, b)
                    st.inc(tensor.transpose(BANK(b, 128),
                                            lring[u % 2][:, kc * 128:(kc + 1) * 128],
                                            eye[:, :]), "ct")
                    V[f"A_tp_{u}_{kc}"] = st.cnt("ct")

                @block.vector
                def _(vector, u=u, kc=kc):
                    st.wait(vector, "ct", V[f"A_tp_{u}_{kc}"])
                    cp = vector.tensor_copy(
                        xfm[:, kc * S + u * 128: kc * S + (u + 1) * 128],
                        BANK(kc % 4, 128))
                    st.inc(cp, "cv")
                    st.bset(kc % 4, "cv", st.cnt("cv"))
            V[f"A_tp_{u}"] = st.cnt("ct")

        for tt in range(NTT + 1):
            if tt < NTT:
                sem = f"ld{tt % 2}"

                @block.sync
                def _(sync, tt=tt, sem=sem):
                    if tt >= 2:
                        st.wait(sync, "cv", V[f"A{tt - 2}_ln"])
                    st.inc(sync.dma_start(out=xring[tt % 2][:],
                                          in_=ap_x[tt * 128:(tt + 1) * 128, :]),
                           sem, 16)
                V[f"A_ld_{tt}"] = st.cnt(sem)

                def pre(vector, tt=tt, sem=sem):
                    st.wait(vector, sem, V[f"A_ld_{tt}"])
                ln_stats(tt % 2, xring[tt % 2][:, :], f"A{tt}", pre)
                ln_rstd(tt % 2, f"A{tt}")
            if tt >= 1:
                a_tail(tt - 1)
        V["A_DONE"] = st.cnt("cv")

        if upto == "A":
            _finish(nc, block, st, ap_dbg, dump, locals(), ap_out, xring)
            s_b.close(); s_bc.close()
            return nc

        def emit_rope(c):
            is_k = c >= 8
            sl = c % 2

            @block.vector
            def _(vector, c=c, is_k=is_k, sl=sl):
                st.wait(vector, f"swp{c % 2}", V[f"B_sw_{c}"])
                if c == 0:
                    st.wait(vector, "cst", V["CST_ALL"])
                vector.drain()
                vector.tensor_tensor(qraw[sl][:, :], qraw[sl][:, :], costab[:, :],
                                     ALU.mult)
                vector.tensor_tensor(qswp[sl][:, :], qswp[sl][:, :], ssintab[:, :],
                                     ALU.mult)
                vector.drain()
                if not is_k:
                    st.inc(vector.tensor_tensor(q_sb[:, c * S:(c + 1) * S],
                                                qraw[sl][:, :], qswp[sl][:, :],
                                                ALU.add), "cv")
                else:
                    for half, kvh in ((0, 2 * (c - 8)), (1, 2 * (c - 8) + 1)):
                        lo = half * 64
                        vector.tensor_tensor(k2_sb[0:64, kvh * S:(kvh + 1) * S],
                                             qraw[sl][lo:lo + 64, :],
                                             qswp[sl][lo:lo + 64, :], ALU.add)
                        cp = vector.tensor_tensor(k2_sb[64:128, kvh * S:(kvh + 1) * S],
                                                  qraw[sl][lo:lo + 64, :],
                                                  qswp[sl][lo:lo + 64, :], ALU.add)
                    st.inc(cp, "cv")
            V[f"B_rope_{c}"] = st.cnt("cv")

        for c in range(10):
            is_k = c >= 8
            sl = c % 2
            @block.tensor
            def _(tensor, c=c, is_k=is_k):
                if c == 0:
                    st.wait(tensor, "ws0", V["WS_QKVO"])
                    st.wait(tensor, "cv", V["A_DONE"])
                for tb in range(NTB):
                    b = 4 + tb
                    st.bwait(tensor, b)
                    for kc in range(NKC):
                        if is_k:
                            lhs = wk_sb[:, kc * 256 + (c - 8) * 128: kc * 256 + (c - 7) * 128]
                        else:
                            lhs = wq_sb[:, kc * D + c * 128: kc * D + (c + 1) * 128]
                        mm = tensor.matmul(BANK(b),
                                           lhs,
                                           xfm[:, kc * S + tb * 512: kc * S + (tb + 1) * 512],
                                           start=(kc == 0), stop=(kc == NKC - 1))
                    st.inc(mm, "ct")
                    V[f"B_mm_{c}_{tb}"] = st.cnt("ct")

            @block.vector
            def _(vector, c=c, sl=sl):
                vector.drain()
                for tb in range(NTB):
                    st.wait(vector, "ct", V[f"B_mm_{c}_{tb}"])
                    cp = vector.tensor_copy(qraw[sl][:, tb * 512:(tb + 1) * 512],
                                            BANK(4 + tb))
                    st.inc(cp, "cv")
                    st.bset(4 + tb, "cv", st.cnt("cv"))
            V[f"B_ev_{c}"] = st.cnt("cv")

            @block.scalar
            def _(scalar, c=c, sl=sl):
                st.wait(scalar, "cv", V[f"B_ev_{c}"])
                sw = st.sems[f"swp{c % 2}"]
                scalar.dma_start(out=qswp[sl][0:32], in_=qraw[sl][32:64]).then_inc(sw, 16)
                scalar.dma_start(out=qswp[sl][32:64], in_=qraw[sl][0:32]).then_inc(sw, 16)
                scalar.dma_start(out=qswp[sl][64:96], in_=qraw[sl][96:128]).then_inc(sw, 16)
                scalar.dma_start(out=qswp[sl][96:128], in_=qraw[sl][64:96]).then_inc(sw, 16)
            st.c[f"swp{c % 2}"] = st.cnt(f"swp{c % 2}") + 64
            V[f"B_sw_{c}"] = st.cnt(f"swp{c % 2}")

            if c >= 1:
                emit_rope(c - 1)
        emit_rope(9)

        for tt in range(NTT):
            @block.tensor
            def _(tensor, tt=tt):
                b = 4 + tt % 4
                st.bwait(tensor, b)
                for kc in range(NKC):
                    mm = tensor.matmul(BANK(b, 256),
                                       xfm[:, kc * S + tt * 128: kc * S + (tt + 1) * 128],
                                       wv_sb[:, kc * 256:(kc + 1) * 256],
                                       start=(kc == 0), stop=(kc == NKC - 1))
                st.inc(mm, "ct")
                V[f"B_vmm_{tt}"] = st.cnt("ct")

            @block.vector
            def _(vector, tt=tt):
                st.wait(vector, "ct", V[f"B_vmm_{tt}"])
                if tt == 0:
                    st.wait(vector, "cg", V["CG_INIT"])
                for kvh in range(KVH):
                    cp = vector.tensor_copy(
                        v_sb[:, tt * KVH * 65 + kvh * 65: tt * KVH * 65 + kvh * 65 + 64],
                        BANK(4 + tt % 4, 256)[:, kvh * 64:(kvh + 1) * 64])
                st.inc(cp, "cv")
                st.bset(4 + tt % 4, "cv", st.cnt("cv"))
        V["B_DONE"] = st.cnt("cv")

        if upto == "B":
            _finish(nc, block, st, ap_dbg, dump, locals(), ap_out, xring)
            s_b.close(); s_bc.close()
            return nc

        s_b.close()
        s_cd = ExitStack()
        yall = s_cd.enter_context(nc.sbuf_tensor("a_yall", [128, 8 * S], BF16))
        s_c2 = ExitStack()
        c2b = lambda n, sh, dt=F32: s_c2.enter_context(nc.sbuf_tensor(n, sh, dt))
        masks_sb = c2b("a_masks", [128, 2048], BF16)
        attw = c2b("a_attw", [128, 16 * 1024], BF16)
        zstr = lring[0]
        ytmp_att = lring[1]

        @block.scalar
        def _(scalar):
            st.inc(scalar.dma_start(out=masks_sb[:], in_=ap_masks[:, :]), "cst", 16)
        V["CST2"] = st.cnt("cst")

        sc_i = 0
        pending_zb = []
        pair_guard = {}
        for hp in range(8):
            kvh = hp // 2
            for qb in range(NTB):
                n_kc = 4 * (qb + 1)
                kcs = list(range(4 * qb, 4 * qb + 4)) + list(range(0, 4 * qb))
                ct0 = st.cnt("ct")
                cs0 = st.cnt("cs")
                st.c["ct"] = ct0 + n_kc
                st.c["cs"] = cs0 + n_kc
                for i, kc in enumerate(kcs):
                    V[f"C_sc_{hp}_{qb}_{kc}"] = ct0 + i + 1
                    V[f"C_exp_{hp}_{qb}_{kc}"] = cs0 + i + 1

                @block.tensor
                def _(tensor, hp=hp, qb=qb, kcs=kcs, kvh=kvh, ct0=ct0, cs0=cs0):
                    if hp == 0 and qb == 0:
                        st.wait(tensor, "cv", V["B_DONE"])
                        st.wait(tensor, "cst", V["CST_ALL"])
                    for i, kc in enumerate(kcs):
                        pair = (sc_i + i) % 2
                        b0 = 0 if pair == 0 else 2
                        if i >= 2:
                            tensor.wait_ge(st.sems["cs"], cs0 + i - 1)
                        elif pair_guard.get(pair) is not None:
                            tensor.wait_ge(st.sems["cs"], pair_guard[pair])
                        tensor.matmul(BANK(b0),
                                      k2_sb[0:64, kvh * S + kc * 128: kvh * S + (kc + 1) * 128],
                                      q_sb[0:64, hp * S + qb * 512: hp * S + (qb + 1) * 512],
                                      start=True, stop=True, tile_position=(0, 0))
                        tensor.matmul(
                            BANK(b0 + 1),
                            k2_sb[64:128, kvh * S + kc * 128: kvh * S + (kc + 1) * 128],
                            q_sb[64:128, hp * S + qb * 512: hp * S + (qb + 1) * 512],
                            start=True, stop=True,
                            tile_position=(64, 0)).then_inc(st.sems["ct"], 1)

                @block.scalar
                def _(scalar, hp=hp, qb=qb, kcs=kcs, ct0=ct0):
                    pv = V.get("C_prev_attv")
                    if pv is not None:
                        st.wait(scalar, "ct", pv)
                    for i, kc in enumerate(kcs):
                        pair = (sc_i + i) % 2
                        b0 = 0 if pair == 0 else 2
                        scalar.wait_ge(st.sems["ct"], ct0 + i + 1)
                        scalar.activation(
                            out=attw[:, kc * 1024:(kc + 1) * 1024],
                            in_=psum[:, b0 * 512: b0 * 512 + 1024],
                            func=AF.Exp, bias=-10.0,
                            scale=0.125).then_inc(st.sems["cs"], 1)

                pair_guard[(sc_i + n_kc - 2) % 2] = cs0 + n_kc - 1
                pair_guard[(sc_i + n_kc - 1) % 2] = cs0 + n_kc
                sc_i += n_kc

                @block.vector
                def _(vector, hp=hp, qb=qb):
                    st.wait(vector, "cst", V["CST2"])
                    st.wait(vector, "cs", V[f"C_exp_{hp}_{qb}_{4 * qb + 3}"])
                    for r in range(4):
                        kc = 4 * qb + r
                        for hh in range(2):
                            mk = vector.tensor_tensor(
                                attw[:, kc * 1024 + hh * 512: kc * 1024 + hh * 512 + 512],
                                attw[:, kc * 1024 + hh * 512: kc * 1024 + hh * 512 + 512],
                                masks_sb[:, r * 512:(r + 1) * 512], ALU.mult)
                    st.inc(mk, "cv")
                V[f"C_mask_{hp}_{qb}"] = st.cnt("cv")

                last_kc = kcs[-1]
                if pending_zb:
                    fn = pending_zb.pop(0)
                    fn()

                @block.tensor
                def _(tensor, hp=hp, qb=qb, n_kc=n_kc, last_kc=last_kc, kvh=kvh):
                    st.wait(tensor, "cv", V[f"C_mask_{hp}_{qb}"])
                    st.wait(tensor, "cs", V[f"C_exp_{hp}_{qb}_{last_kc}"])
                    st.bwait(tensor, 4)
                    st.bwait(tensor, 5)
                    for i_kc in range(n_kc):
                        for hh in range(2):
                            mm = tensor.matmul(
                                BANK(4 + hh)[0:65, :],
                                v_sb[:, i_kc * KVH * 65 + kvh * 65: i_kc * KVH * 65 + kvh * 65 + 65],
                                attw[:, i_kc * 1024 + hh * 512: i_kc * 1024 + hh * 512 + 512],
                                start=(i_kc == 0), stop=(i_kc == n_kc - 1))
                            st.inc(mm, "ct") if i_kc == n_kc - 1 else None
                V[f"C_y_{hp}_{qb}_0"] = st.cnt("ct") - 1
                V[f"C_y_{hp}_{qb}_1"] = st.cnt("ct")

                @block.vector
                def _(vector, hp=hp, qb=qb):
                    st.wait(vector, "ct", V[f"C_y_{hp}_{qb}_1"])
                    g = st.bankg.get(100)
                    if g is not None:
                        st.wait(vector, g[0], g[1])
                    vector.drain()
                    vector.tensor_copy(zstr[0:1, 0:512], BANK(4)[64:65, :])
                    vector.tensor_copy(zstr[0:1, 512:1024], BANK(5)[64:65, :])
                    vector.tensor_copy(ytmp_att[0:64, 0:512], BANK(4)[0:64, :])
                    cp = vector.tensor_copy(ytmp_att[64:128, 0:512], BANK(5)[0:64, :])
                    st.inc(cp, "cv")
                    st.bset(4, "cv", st.cnt("cv"))
                    st.bset(5, "cv", st.cnt("cv"))
                    vector.drain()
                    st.inc(vector.reciprocal(zstr[0:1, 0:1024], zstr[0:1, 0:1024]),
                           "cv")
                V[f"C_z_{hp}_{qb}"] = st.cnt("cv")

                def make_zb(hp=hp, qb=qb):
                    def emit():
                        @block.tensor
                        def _(tensor):
                            st.wait(tensor, "cv", V[f"C_z_{hp}_{qb}"])
                            st.bwait(tensor, 6)
                            tensor.matmul(
                                BANK(6)[0:64, :], ind2_sb[0:1, 0:64],
                                zstr[0:1, 0:512],
                                start=True, stop=True, tile_position=(0, 0))
                            st.inc(tensor.matmul(
                                BANK(6)[64:128, :], ind2_sb[0:1, 0:64],
                                zstr[0:1, 512:1024],
                                start=True, stop=True, tile_position=(0, 64)), "ct")
                        V[f"C_zb_{hp}_{qb}"] = st.cnt("ct")
                        st.bset(100, "ct", st.cnt("ct"))

                        @block.vector
                        def _(vector):
                            st.wait(vector, "ct", V[f"C_zb_{hp}_{qb}"])
                            st.inc(vector.tensor_tensor(
                                yall[:, hp * S + qb * 512: hp * S + (qb + 1) * 512],
                                ytmp_att[:, 0:512],
                                BANK(6), ALU.mult), "cv")
                            st.bset(6, "cv", st.cnt("cv"))
                    return emit
                pending_zb.append(make_zb())
                V["C_prev_attv"] = st.cnt("ct")
        while pending_zb:
            pending_zb.pop(0)()
        V["C_DONE"] = st.cnt("cv")

        if upto == "C":
            _finish(nc, block, st, ap_dbg, dump, locals(), ap_out, xring)
            s_c2.close(); s_cd.close(); s_bc.close()
            return nc

        s_c2.close()
        s_d = ExitStack()
        db_ = lambda n, sh, dt=F32: s_d.enter_context(nc.sbuf_tensor(n, sh, dt))
        x1t = [db_(f"d_x1_{i}", [128, D]) for i in range(3)]
        gring8 = db_("d_g8", [128, D])
        gev = db_("d_gev", [128, E])
        d_m1 = db_("d_m1", [128, 1])
        d_negm1 = db_("d_negm1", [128, 1])
        d_ismax = db_("d_ismax", [128, E])
        d_l2 = db_("d_l2", [128, E])
        d_m2 = db_("d_m2", [128, 1])
        d_expw = db_("d_expw", [128, E])
        d_maskw = db_("d_maskw", [128, E])
        d_wm = db_("d_wm", [128, E])
        d_zs = db_("d_zs", [128, 1])
        d_zr = db_("d_zr", [128, 1])
        ln3o = [lring[1], db_("d_ln3b", [128, D])]

        def d_head(tt):
            sem = f"ld{tt % 2}"

            @block.sync
            def _(sync, tt=tt, sem=sem):
                if tt == 0:
                    st.wait(sync, "cv", V["C_DONE"])
                if tt >= 2:
                    st.wait(sync, "cv", V[f"D_x1_{tt - 2}"])
                st.inc(sync.dma_start(out=xring[tt % 2][:],
                                      in_=ap_x[tt * 128:(tt + 1) * 128, :]), sem, 16)
            V[f"D_ld_{tt}"] = st.cnt(sem)

            ob = 0 if tt % 2 == 0 else 2
            @block.tensor
            def _(tensor, tt=tt, ob=ob):
                if tt == 0:
                    st.wait(tensor, "cv", V["C_DONE"])
                st.bwait(tensor, ob)
                st.bwait(tensor, ob + 1)
                for db in range(2):
                    for kc in range(NKC):
                        mm = tensor.matmul(
                            BANK(ob + db),
                            yall[:, kc * S + tt * 128: kc * S + (tt + 1) * 128],
                            wo_sb[:, kc * D + db * 512: kc * D + db * 512 + 512],
                            start=(kc == 0), stop=(kc == NKC - 1))
                st.inc(mm, "ct")
            V[f"D_omm_{tt}"] = st.cnt("ct")

            def preo(vector, tt=tt):
                st.wait(vector, "ct", V[f"D_omm_{tt}"])
            ln_stats(tt % 2, psum[:, ob * 512: ob * 512 + 1024], f"D2_{tt}", preo)
            ln_rstd(tt % 2, f"D2_{tt}")

        def d_tail(u):
            pu = u % 2
            ob = 0 if u % 2 == 0 else 2
            attn_ap = psum[:, ob * 512: ob * 512 + 1024]
            L = lnS[pu]

            @block.vector
            def _(vector, u=u, pu=pu, attn_ap=attn_ap, L=L):
                st.wait(vector, "cs", V[f"D2_{u}_rstd"])
                st.wait(vector, f"ld{u % 2}", V[f"D_ld_{u}"])
                if u >= 3:
                    st.wait(vector, f"sp{u % 3}", V[f"D_sp_{u - 3}"])
                vector.tensor_scalar(out=lring[0][:, :], in0=attn_ap,
                                     scalar1=L.tmp2[:, 0:1], scalar2=L.sr[:, :],
                                     op0=ALU.subtract, op1=ALU.mult)
                vector.drain()
                st.inc(vector.tensor_tensor(x1t[u % 3][:, :], lring[0][:, :],
                                            xring[u % 2][:, :], ALU.add), "cv")
            V[f"D_x1_{u}"] = st.cnt("cv")
            st.bset(ob, "cv", st.cnt("cv"))
            st.bset(ob + 1, "cv", st.cnt("cv"))

            @block.scalar
            def _(scalar, u=u):
                st.wait(scalar, "cv", V[f"D_x1_{u}"])
                st.inc(scalar.dma_start(out=ap_x1s[u * 128:(u + 1) * 128, :],
                                        in_=x1t[u % 3][:, :]), f"sp{u % 3}", 16)
            V[f"D_sp_{u}"] = st.cnt(f"sp{u % 3}")

            ln_stats(pu, x1t[u % 3][:, :], f"D3_{u}", None)
            ln_rstd(pu, f"D3_{u}")

            def pre3(vector, u=u):
                if u >= 2:
                    st.wait(vector, "ct", V[f"D_tpall_{u - 2}"])
            ln_norm(pu, x1t[u % 3][:, :], ln3o[pu][:, :], f"D3_{u}", pre3)

            for kc in range(NKC):
                @block.tensor
                def _(tensor, u=u, kc=kc, pu=pu):
                    if kc == 0:
                        st.wait(tensor, "cv", V[f"D3_{u}_ln"])
                    b = 4 + kc % 2
                    st.bwait(tensor, b)
                    st.inc(tensor.transpose(BANK(b, 128),
                                            ln3o[pu][:, kc * 128:(kc + 1) * 128],
                                            eye[:, :]), "ct")
                    V[f"D_tp_{u}_{kc}"] = st.cnt("ct")

                @block.vector
                def _(vector, u=u, kc=kc):
                    if kc == 0 and u >= 1:
                        st.wait(vector, "ct", V[f"D_gmm_{u - 1}"])
                    st.wait(vector, "ct", V[f"D_tp_{u}_{kc}"])
                    vector.tensor_copy(
                        xfm[:, kc * S + u * 128: kc * S + (u + 1) * 128],
                        BANK(4 + kc % 2, 128))
                    cp = vector.tensor_copy(gring8[:, kc * 128:(kc + 1) * 128],
                                            BANK(4 + kc % 2, 128))
                    st.inc(cp, "cv")
                    st.bset(4 + kc % 2, "cv", st.cnt("cv"))
                V[f"D_tpev_{u}_{kc}"] = st.cnt("cv")
            V[f"D_tpall_{u}"] = st.cnt("ct")
            V[f"D_tpev_{u}"] = st.cnt("cv")

            @block.tensor
            def _(tensor, u=u):
                st.wait(tensor, "cv", V[f"D_tpev_{u}"])
                st.bwait(tensor, 6)
                for kc in range(NKC):
                    mm = tensor.matmul(BANK(6, 8), gring8[:, kc * 128:(kc + 1) * 128],
                                       gwT_sb[:, kc * E:(kc + 1) * E],
                                       start=(kc == 0), stop=(kc == NKC - 1))
                st.inc(mm, "ct")
            V[f"D_gmm_{u}"] = st.cnt("ct")

            @block.vector
            def _(vector, u=u):
                st.wait(vector, "ct", V[f"D_gmm_{u}"])
                vector.drain()
                vector.tensor_copy(gev[:, :], BANK(6, 8))
                vector.drain()
                vector.reduce_max(d_m1[:, :], gev[:, :], axis=AX)
                vector.drain()
                vector.tensor_scalar(out=d_ismax[:, :], in0=gev[:, :],
                                     scalar1=d_m1[:, :], scalar2=None, op0=ALU.is_ge)
                vector.tensor_scalar(out=d_negm1[:, :], in0=d_m1[:, :],
                                     scalar1=-1.0, scalar2=None, op0=ALU.mult)
                vector.drain()
                vector.scalar_tensor_tensor(out=d_l2[:, :], in0=d_ismax[:, :],
                                            scalar=-1e30, in1=gev[:, :],
                                            op0=ALU.mult, op1=ALU.add)
                vector.drain()
                vector.reduce_max(d_m2[:, :], d_l2[:, :], axis=AX)
                vector.drain()
                st.inc(vector.tensor_scalar(out=d_maskw[:, :], in0=gev[:, :],
                                            scalar1=d_m2[:, :], scalar2=None,
                                            op0=ALU.is_ge), "cv")
                st.bset(6, "cv", st.cnt("cv"))
            V[f"D_cm1_{u}"] = st.cnt("cv")

            @block.scalar
            def _(scalar, u=u):
                st.wait(scalar, "cv", V[f"D_cm1_{u}"])
                st.inc(scalar.activation(out=d_expw[:, :], in_=gev[:, :], func=AF.Exp,
                                         bias=d_negm1[:, :], scale=1.0), "cs")
            V[f"D_cexp_{u}"] = st.cnt("cs")

            @block.vector
            def _(vector, u=u):
                st.wait(vector, "cs", V[f"D_cexp_{u}"])
                vector.tensor_tensor(d_wm[:, :], d_expw[:, :], d_maskw[:, :], ALU.mult)
                vector.drain()
                vector.reduce_sum(d_zs[:, :], d_wm[:, :], axis=AX)
                vector.drain()
                vector.reciprocal(d_zr[:, :], d_zs[:, :])
                vector.drain()
                st.inc(vector.tensor_scalar(
                    out=comb_sb[:, u * (E + 1): u * (E + 1) + E],
                    in0=d_wm[:, :], scalar1=d_zr[:, :], scalar2=None,
                    op0=ALU.mult), "cv")
            V[f"D_comb_{u}"] = st.cnt("cv")
            V[f"D_gev_{u}"] = st.cnt("cv")

        for tt in range(NTT + 1):
            if tt < NTT:
                d_head(tt)
            if tt >= 1:
                d_tail(tt - 1)

        for tt in range(NTT):
            @block.tensor
            def _(tensor, tt=tt):
                if tt == 0:
                    st.wait(tensor, "cv", V[f"D_comb_{NTT - 1}"])
                    st.wait(tensor, "cg", V["CG_INIT"])
                st.bwait(tensor, 7)
                st.inc(tensor.transpose(BANK(7, 128)[0:E + 1, :],
                                        comb_sb[:, tt * (E + 1):(tt + 1) * (E + 1)],
                                        eye[:, :]), "ct")
            V[f"D_cT_{tt}"] = st.cnt("ct")

            @block.vector
            def _(vector, tt=tt):
                st.wait(vector, "ct", V[f"D_cT_{tt}"])
                st.inc(vector.tensor_copy(cT_sb[:, tt * 128:(tt + 1) * 128],
                                          BANK(7, 128)[0:E + 1, :]), "cv")
                st.bset(7, "cv", st.cnt("cv"))
        V["D_DONE"] = st.cnt("cv")

        if upto == "D":
            _finish(nc, block, st, ap_dbg, dump, locals(), ap_out, xring)
            s_d.close(); s_c2.close(); s_cd.close(); s_bc.close()
            return nc

        s_d.close()
        s_cd.close()
        s_bc.close()
        w1r = [sb(f"f_w1_{i}", [128, NKC * 2048], BF16) for i in range(2)]
        w2r = [sb(f"f_w2_{i}", [128, 16 * D], BF16) for i in range(2)]
        h_sb = sb("f_h", [128, 16 * 256], BF16)
        ytr = [sb(f"f_yt_{i}", [128, 512]) for i in range(2)]
        x1h = [sb(f"g_x1_{i}", [128, 512]) for i in range(2)]

        def emit_wload(pi):
            ex, hf = pi // 2, pi % 2
            slot = pi % 2
            wsem = f"ws{slot}"

            @block.gpsimd
            def _(gpsimd):
                if pi >= 2:
                    st.wait(gpsimd, "ct", V[f"E_done_{pi - 2}"])
                else:
                    st.wait(gpsimd, "cv", V["D_DONE"])
                    for sx in range(3):
                        st.wait(gpsimd, f"sp{sx}", st.cnt(f"sp{sx}"))
                st.inc(gpsimd.dma_start(
                    out=w1r[slot][:].rearrange("p (kc n) -> p kc n", n=2048),
                    in_=ap_w1[ex].rearrange("(kc p) n -> p kc n", p=128)[:, :, hf * 2048:(hf + 1) * 2048]),
                    wsem, 16)
                st.inc(gpsimd.dma_start(
                    out=w2r[slot][:].rearrange("p (fc n) -> p fc n", n=D),
                    in_=ap_w2[ex].rearrange("(fc p) n -> p fc n", p=128)[:, hf * 16:(hf + 1) * 16, :]),
                    wsem, 16)
            V[f"E_wld_{pi}"] = st.cnt(wsem)

        emit_wload(0)
        emit_wload(1)

        for tt in range(NTT):
            for db in range(2):
                yslot = (2 * tt + db) % 2
                b = 4 + (2 * tt + db) % 3
                @block.tensor
                def _(tensor, tt=tt, db=db, b=b):
                    if tt == 0 and db == 0:
                        st.wait(tensor, "cv", V["D_DONE"])
                    st.bwait(tensor, b)
                    st.inc(tensor.matmul(BANK(b), cT_sb[:, tt * 128:(tt + 1) * 128],
                                         b2aug_sb[:, db * 512:(db + 1) * 512],
                                         start=True, stop=True), "ct")
                V[f"E_b2_{tt}_{db}"] = st.cnt("ct")

                @block.vector
                def _(vector, tt=tt, db=db, b=b, yslot=yslot):
                    st.wait(vector, "ct", V[f"E_b2_{tt}_{db}"])
                    g = st.bankg.get(200 + yslot)
                    if g is not None:
                        st.wait(vector, g[0], g[1])
                    st.inc(vector.tensor_copy(ytr[yslot][:, :], BANK(b)), "cv")
                    st.bset(b, "cv", st.cnt("cv"))
                V[f"E_b2ev_{tt}_{db}"] = st.cnt("cv")

                @block.gpsimd
                def _(gpsimd, tt=tt, db=db, yslot=yslot):
                    st.wait(gpsimd, "cv", V[f"E_b2ev_{tt}_{db}"])
                    st.inc(gpsimd.dma_start(
                        out=ap_facc[tt * 128:(tt + 1) * 128, db * 512:(db + 1) * 512],
                        in_=ytr[yslot][:, :]), f"fa{yslot}", 16)
                st.bset(200 + yslot, f"fa{yslot}", st.cnt(f"fa{yslot}"))

        yslot_i = 0
        NTB2 = 8
        hpair_guard = {}

        for pi in range(18):
            ex, hf = pi // 2, pi % 2
            slot = pi % 2
            wsem = f"ws{slot}"
            cs0 = st.cnt("cs")
            st.c["cs"] = cs0 + 8 * NTB2
            GEL = lambda tb, fp: cs0 + tb * 8 + fp + 1
            for tb in range(NTB2):
                for fp in range(8):
                    ps_ = fp % 2
                    @block.tensor
                    def _(tensor, pi=pi, tb=tb, fp=fp, ps_=ps_, slot=slot,
                          wsem=wsem, cs0=cs0):
                        if tb == 0 and fp == 0:
                            st.wait(tensor, wsem, V[f"E_wld_{pi}"])
                        gi = tb * 8 + fp - 2
                        if gi >= 0:
                            tensor.wait_ge(st.sems["cs"], cs0 + gi + 1)
                        elif hpair_guard.get(ps_) is not None:
                            tensor.wait_ge(st.sems["cs"], hpair_guard[ps_])
                        for fc in (2 * fp, 2 * fp + 1):
                            b = 2 * ps_ + (fc % 2)
                            for kc in range(NKC):
                                mm = tensor.matmul(
                                    BANK(b, 256),
                                    w1r[slot][:, kc * 2048 + fc * 128: kc * 2048 + (fc + 1) * 128],
                                    xfm[:, kc * S + tb * 256: kc * S + (tb + 1) * 256],
                                    start=(kc == 0), stop=(kc == NKC - 1))
                        st.inc(mm, "ct")
                        V[f"E_hmm_{pi}_{tb}_{fp}"] = st.cnt("ct")

                for tt2 in range(2):
                    tt_g = tb * 2 + tt2
                    for db in range(2):
                        b = 4 + (tt2 * 2 + db) % 3
                        @block.tensor
                        def _(tensor, pi=pi, tb=tb, tt2=tt2, db=db, b=b,
                              slot=slot, cs0=cs0):
                            st.bwait(tensor, b)
                            for fc in range(16):
                                if tt2 == 0 and db == 0:
                                    tensor.wait_ge(st.sems["cs"], GEL(tb, fc // 2))
                                mm = tensor.matmul(
                                    BANK(b),
                                    h_sb[:, fc * 256 + tt2 * 128: fc * 256 + tt2 * 128 + 128],
                                    w2r[slot][:, fc * D + db * 512: fc * D + db * 512 + 512],
                                    start=(fc == 0), stop=(fc == 15))
                            st.inc(mm, "ct")
                        V[f"E_ymm_{pi}_{tb}_{tt2}_{db}"] = st.cnt("ct")

                        yslot = yslot_i % 2
                        yslot_i += 1
                        @block.vector
                        def _(vector, pi=pi, ex=ex, tb=tb, tt2=tt2, db=db, b=b,
                              yslot=yslot, tt_g=tt_g):
                            st.wait(vector, "ct", V[f"E_ymm_{pi}_{tb}_{tt2}_{db}"])
                            g = st.bankg.get(200 + yslot)
                            if g is not None:
                                st.wait(vector, g[0], g[1])
                            if ex < E:
                                op = vector.tensor_scalar(
                                    out=ytr[yslot][:, :], in0=BANK(b),
                                    scalar1=comb_sb[:, tt_g * (E + 1) + ex: tt_g * (E + 1) + ex + 1],
                                    scalar2=None, op0=ALU.mult)
                            else:
                                op = vector.tensor_copy(ytr[yslot][:, :], BANK(b))
                            st.inc(op, "cv")
                            st.bset(b, "cv", st.cnt("cv"))
                        V[f"E_yev_{pi}_{tb}_{tt2}_{db}"] = st.cnt("cv")

                        @block.gpsimd
                        def _(gpsimd, pi=pi, tb=tb, tt2=tt2, db=db, yslot=yslot,
                              tt_g=tt_g):
                            st.wait(gpsimd, "cv", V[f"E_yev_{pi}_{tb}_{tt2}_{db}"])
                            st.inc(gpsimd.dma_start(
                                out=ap_facc[tt_g * 128:(tt_g + 1) * 128,
                                            db * 512:(db + 1) * 512],
                                in_=ytr[yslot][:, :], accum_op=ALU.add),
                                f"fa{yslot}", 16)
                        st.bset(200 + yslot, f"fa{yslot}", st.cnt(f"fa{yslot}"))
                        if pi == 17 and db == 1:
                            V[f"E_tile_done_{tt_g}"] = (st.cnt("fa0"), st.cnt("fa1"))

            @block.scalar
            def _(scalar, pi=pi, ex=ex, hf=hf, cs0=cs0):
                for tb in range(NTB2):
                    for fp in range(8):
                        ps_ = fp % 2
                        scalar.wait_ge(st.sems["ct"], V[f"E_hmm_{pi}_{tb}_{fp}"])
                        scalar.activation(
                            out=h_sb[:, fp * 512:(fp + 1) * 512].rearrange(
                                "p (b n) -> p b n", n=256),
                            in_=psum[:, ps_ * 1024: ps_ * 1024 + 1024].rearrange(
                                "p (b n) -> p b n", n=512)[:, :, 0:256],
                            func=AF.Gelu,
                            bias=b1_sb[:, ex * 32 + hf * 16 + 2 * fp: ex * 32 + hf * 16 + 2 * fp + 1],
                            scale=1.0).then_inc(st.sems["cs"], 1)
            hpair_guard[0] = cs0 + 8 * NTB2 - 1
            hpair_guard[1] = cs0 + 8 * NTB2

            V[f"E_done_{pi}"] = st.cnt("ct")
            if pi + 2 < 18:
                emit_wload(pi + 2)

        if upto == "E":
            _finish(nc, block, st, ap_dbg, dump, locals(), ap_out, xring)
            return nc

        fring = [xring[0], xring[1]]
        o_t = [lring[0], lring[1]]
        x1p = [[x1h[0], x1h[1]], [ytr[0], ytr[1]]]
        FA_TOT = {x: st.cnt(f"fa{x}") for x in range(2)}
        SP_TOT = {x: st.cnt(f"sp{x}") for x in range(3)}

        def f_head(tt):
            fs = f"lf{tt % 2}"
            xs = f"lx{tt % 2}"

            @block.sync
            def _(sync, tt=tt, fs=fs):
                fa0v, fa1v = V[f"E_tile_done_{tt}"]
                st.wait(sync, "fa0", fa0v)
                st.wait(sync, "fa1", fa1v)
                if tt == 0:
                    for x in range(3):
                        st.wait(sync, f"sp{x}", SP_TOT[x])
                if tt >= 2:
                    st.wait(sync, "cv", V[f"F_o_{tt - 2}"])
                st.inc(sync.dma_start(out=fring[tt % 2][:],
                                      in_=ap_facc[tt * 128:(tt + 1) * 128, :]), fs, 16)
            V[f"F_lf_{tt}"] = st.cnt(fs)

            @block.scalar
            def _(scalar, tt=tt, xs=xs):
                if tt == 0:
                    st.wait(scalar, "fa0", FA_TOT[0])
                    st.wait(scalar, "fa1", FA_TOT[1])
                if tt >= 2:
                    st.wait(scalar, "cv", V[f"F_o_{tt - 2}"])
                st.inc(scalar.dma_start(out=x1p[tt % 2][0][:, :],
                                        in_=ap_x1s[tt * 128:(tt + 1) * 128, 0:512]), xs, 16)
                st.inc(scalar.dma_start(out=x1p[tt % 2][1][:, :],
                                        in_=ap_x1s[tt * 128:(tt + 1) * 128, 512:1024]), xs, 16)
            V[f"F_lx_{tt}"] = st.cnt(xs)

            def pref(vector, tt=tt, fs=fs):
                st.wait(vector, fs, V[f"F_lf_{tt}"])
            ln_stats(tt % 2, fring[tt % 2][:, :], f"F_{tt}", pref)
            ln_rstd(tt % 2, f"F_{tt}")

        def f_tail(u):
            pu = u % 2
            L = lnS[pu]

            @block.vector
            def _(vector, u=u, pu=pu, L=L):
                st.wait(vector, "cs", V[f"F_{u}_rstd"])
                st.wait(vector, f"lx{pu}", V[f"F_lx_{u}"])
                if u >= 2:
                    st.wait(vector, f"so{u % 2}", V[f"F_st_{u - 2}"])
                vector.tensor_scalar(out=o_t[pu][:, :], in0=fring[pu][:, :],
                                     scalar1=L.tmp2[:, 0:1], scalar2=L.sr[:, :],
                                     op0=ALU.subtract, op1=ALU.mult)
                vector.drain()
                vector.tensor_tensor(o_t[pu][:, 0:512], o_t[pu][:, 0:512],
                                     x1p[pu][0][:, :], ALU.add)
                st.inc(vector.tensor_tensor(o_t[pu][:, 512:1024],
                                            o_t[pu][:, 512:1024],
                                            x1p[pu][1][:, :], ALU.add), "cv")
            V[f"F_o_{u}"] = st.cnt("cv")

            @block.sync
            def _(sync, u=u):
                st.wait(sync, "cv", V[f"F_o_{u}"])
                st.inc(sync.dma_start(out=ap_out[u * 128:(u + 1) * 128, :],
                                      in_=o_t[u % 2][:, :]), f"so{u % 2}", 16)
            V[f"F_st_{u}"] = st.cnt(f"so{u % 2}")

        for tt in range(NTT + 1):
            if tt < NTT:
                f_head(tt)
            if tt >= 1:
                f_tail(tt - 1)

        @block.sync
        def _(sync):
            sync.wait_ge(st.sems["so0"], st.cnt("so0"))
            sync.wait_ge(st.sems["so1"], st.cnt("so1"))

    return nc


def _finish(nc, block, st, ap_dbg, dump, scope, ap_out, xring):
    @block.sync
    def _(sync):
        for k in ["cv", "cs", "ct"]:
            if st.cnt(k):
                sync.wait_ge(st.sems[k], st.cnt(k))
        n = 0
        if ap_dbg is not None:
            name = dump[0]
            buf = scope[name]
            rows = dump[1][0]
            st.inc(sync.dma_start(out=ap_dbg[:, :], in_=buf[0:rows, 0:dump[1][1]]),
                   "st_o", 16)
            n += 1
        for tt in range(NTT):
            st.inc(sync.dma_start(out=ap_out[tt * 128:(tt + 1) * 128, :],
                                  in_=xring[tt % 2][:, :]), "st_o", 16)
        sync.wait_ge(st.sems["st_o"], st.cnt("st_o"))


def make_inputs_per_core(inputs):
    cos2, ssin2 = _host_tables()
    masks = _host_masks()
    eye = np.eye(128, dtype=np.float32)
    cvals = np.zeros((128, 8), dtype=np.float32)
    cvals[:, 0] = EPS
    cvals[:, 1] = 0.125
    cvals[:, 2] = -10.0
    cvals[:, 3] = 1.0
    cvals[:, 4] = 0.0
    cvals[:, 5] = -0.5
    ind2 = np.zeros((2, 128), dtype=np.float32)
    ind2[0, 0:64] = 1.0
    ind2[1, 64:128] = 1.0

    w1 = np.concatenate([np.asarray(inputs["L0_ffn_W1"]),
                         np.asarray(inputs["L0_ffn_sW1"])[None]], axis=0)
    w2 = np.concatenate([np.asarray(inputs["L0_ffn_W2"]),
                         np.asarray(inputs["L0_ffn_sW2"])[None]], axis=0)
    b1 = np.concatenate([np.asarray(inputs["L0_ffn_b1"]),
                         np.asarray(inputs["L0_ffn_sb1"])[None]], axis=0)
    b1c = np.ascontiguousarray(b1.reshape(E + 1, 32, 128).transpose(2, 0, 1)
                               .reshape(128, (E + 1) * 32))
    b2aug = np.ascontiguousarray(
        np.concatenate([np.asarray(inputs["L0_ffn_b2"]),
                        np.asarray(inputs["L0_ffn_sb2"])[None]], axis=0))
    gwT = np.ascontiguousarray(np.asarray(inputs["L0_ffn_gate_w"]).T)

    shared = {
        "Wq": np.ascontiguousarray(np.asarray(inputs["L0_attn_Wq"])),
        "Wk": np.ascontiguousarray(np.asarray(inputs["L0_attn_Wk"])),
        "Wv": np.ascontiguousarray(np.asarray(inputs["L0_attn_Wv"])),
        "Wo": np.ascontiguousarray(np.asarray(inputs["L0_attn_Wo"])),
        "gwT": gwT, "W1": np.ascontiguousarray(w1), "W2": np.ascontiguousarray(w2),
        "b1c": b1c, "b2aug": b2aug, "costab": cos2, "ssintab": ssin2,
        "masks": masks, "eye": eye, "consts": cvals, "ind2": ind2,
    }
    x = np.asarray(inputs["L0_attn_x"])
    in_maps = []
    for c in range(8):
        m = dict(shared)
        m["x"] = np.ascontiguousarray(x[c])
        in_maps.append(m)
    return in_maps


def kernel(**inputs):
    in_maps = make_inputs_per_core(inputs)
    nc = build_program("full")
    res = run_bass_kernel_spmd(nc, in_maps, core_ids=list(range(8)))
    out = np.stack([np.asarray(res.results[c]["out"]) for c in range(8)], axis=0)
    return out.astype(np.float32)
